# version 25
# baseline (speedup 1.0000x reference)
import sys
import numpy as np
import ml_dtypes
import contextlib

sys.path.insert(0, '/opt/trn_rl_repo')
import concourse.bass as bass
import concourse.mybir as mybir
from concourse.alu_op_type import AluOpType as OP
from concourse.bass_utils import run_bass_kernel_spmd

F32 = mybir.dt.float32
BF16 = mybir.dt.bfloat16
I32 = mybir.dt.int32
AF = mybir.ActivationFunctionType
AX = mybir.AxisListType
BFNP = ml_dtypes.bfloat16

D = 1024
H = 16
HD = 64
T = 2048
NTC = T // 128
G3 = 3 * D
WQ = 64
WIN = 256
DF = 2048
SLOTS = 8
RS = SLOTS * G3
EPS = 1e-5
NT = 2 * NTC


def _ap(base, dims, offset):
    a = base.copy()
    a.ap = type(a.ap)([list(d) for d in dims])
    a.offset = offset
    return a


def vch_norm(i):
    return 6 * i + 7 if i < NTC else 103 + 12 * (i - NTC)


def vch_cen(i):
    return 6 * i + 4 if i < NTC else 100 + 12 * (i - NTC)


def vch_var(i):
    return 6 * i + 5 if i < NTC else 101 + 12 * (i - NTC)


def vch_evac(i):
    return 109 + 12 * i if i < NTC else 293 + 4 * (i - NTC)


def sch_sq(i):
    return 2 * i + 1


def sch_sd(i):
    return 2 * i + 2


def _tc_lens(i):
    return (3, 3) if i < NTC else (2, 2)


def tch_A(i):
    c = 0
    for j in range(i):
        a, b = _tc_lens(j)
        c += a + b
    return c + _tc_lens(i)[0]


def tch_B(i):
    return tch_A(i) + _tc_lens(i)[1]


def build_l1():
    nc = bass.Bass("TRN2", target_bir_lowering=False)
    xq = nc.dram_tensor("xq", [T, D], F32, kind="ExternalInput").ap()
    xf = nc.dram_tensor("xf", [T, D], F32, kind="ExternalInput").ap()
    wqkvT = nc.dram_tensor("wqkvT", [D, G3], F32, kind="ExternalInput").ap()
    qwin = nc.dram_tensor("qwin", [H, WQ, DF], BF16, kind="ExternalOutput").ap()
    kvo = nc.dram_tensor("kvo", [2, H, WQ, DF], BF16, kind="ExternalOutput").ap()
    kvf = nc.dram_tensor("kvf", [2, H, WQ, DF], BF16, kind="ExternalOutput").ap()
    ln_tm = nc.dram_tensor("ln_tm", [2, T, D], BF16).ap()

    with contextlib.ExitStack() as ES:
        E = ES.enter_context
        wq_b = E(nc.sbuf_tensor("wq_b", [128, 8 * G3], BF16))
        xt = E(nc.sbuf_tensor("xt", [128, 2 * D], F32))
        lnt = E(nc.sbuf_tensor("lnt", [128, 2 * D], BF16))
        cen = E(nc.sbuf_tensor("cen", [128, 2 * D], F32))
        sqj = E(nc.sbuf_tensor("sqj", [128, D], F32))
        st = E(nc.sbuf_tensor("st", [128, 16], F32))
        ceps = E(nc.sbuf_tensor("ceps", [128, 1], F32))
        fmo = E(nc.sbuf_tensor("fmo", [128, 8 * T], BF16))
        fmf = E(nc.sbuf_tensor("fmf", [128, 8 * T], BF16))
        ring = E(nc.sbuf_tensor("ring", [128, RS], BF16))
        psA = E(nc.psum_tensor("psA", [128, 3 * 512], F32))
        psB = E(nc.psum_tensor("psB", [128, 3 * 512], F32))

        sX0 = E(nc.semaphore("sX0"))
        sX1 = E(nc.semaphore("sX1"))
        sW = E(nc.semaphore("sW"))
        sTW0 = E(nc.semaphore("sTW0"))
        sTW1 = E(nc.semaphore("sTW1"))
        sFMo = E(nc.semaphore("sFMo"))
        sFMf = E(nc.semaphore("sFMf"))
        sRGg = E(nc.semaphore("sRGg"))
        sRGs = E(nc.semaphore("sRGs"))
        vch = E(nc.semaphore("vch"))
        sch = E(nc.semaphore("sch"))
        tch = E(nc.semaphore("tch"))
        block = E(nc.Block())

        goffA = lambda i: ([0, 512, 1024] if i < NTC else [1024, 1536])
        goffB = lambda i: ([1536, 2048, 2560] if i < NTC else [2048, 2560])
        sh_even = [len([sh for sh in (range(48) if g < 2 else range(16, 48)) if sh % 2 == 0]) for g in range(4)]
        sh_odd = [len([sh for sh in (range(48) if g < 2 else range(16, 48)) if sh % 2 == 1]) for g in range(4)]
        gcum = [0]
        scum = [0]
        for g in range(4):
            gcum.append(gcum[-1] + sh_even[g])
            scum.append(scum[-1] + sh_odd[g])

        @block.sync
        def _(sy):
            def load(i):
                src = xq if i < NTC else xf
                tl = i % NTC
                if i >= 2:
                    sy.wait_ge(vch, vch_norm(i - 2))
                    sy.wait_ge(sch, sch_sq(i - 2))
                sy.dma_start(out=xt[:, (i % 2) * D:(i % 2 + 1) * D],
                             in_=src[tl * 128:(tl + 1) * 128, :]).then_inc(sX0 if i % 2 == 0 else sX1, 16)

            def write(j):
                sy.wait_ge(vch, vch_norm(j))
                sy.dma_start(out=ln_tm[j // NTC, (j % NTC) * 128:(j % NTC + 1) * 128, :],
                             in_=lnt[:, (j % 2) * D:(j % 2 + 1) * D]).then_inc(sTW0 if j % 2 == 0 else sTW1, 16)

            for i in range(NTC):
                load(i)
                if i >= 1:
                    write(i - 1)
            write(NTC - 1)
            sy.wait_ge(sTW0, 16 * (NTC // 2))
            sy.wait_ge(sTW1, 16 * (NTC // 2))
            for k in range(8):
                sy.dma_start(out=fmo[:, k * T:(k + 1) * T],
                             in_=ln_tm[0, :, k * 128:(k + 1) * 128], transpose=True).then_inc(sFMo, 16)
            def rg_grp(grp):
                own = grp < 2
                sy.wait_ge(vch, vch_evac(grp * 8 + 7))
                if grp >= 1:
                    sy.wait_ge(sRGs, 16 * scum[grp])
                for sh in [sh for sh in (range(48) if own else range(16, 48)) if sh % 2 == 1]:
                    _emit_regroup(sy, sh, grp, ring, qwin, kvo, kvf, sRGs)

            for i in range(NTC, NT):
                load(i)
                if i >= NTC + 1:
                    write(i - 1)
                if i == NTC + 9:
                    rg_grp(0)
            write(NT - 1)
            rg_grp(1)
            sy.wait_ge(sTW0, 16 * (NT // 2))
            sy.wait_ge(sTW1, 16 * (NT // 2))
            for k in range(8):
                sy.dma_start(out=fmf[:, k * T:(k + 1) * T],
                             in_=ln_tm[1, :, k * 128:(k + 1) * 128], transpose=True).then_inc(sFMf, 16)
            rg_grp(2)
            rg_grp(3)
            sy.wait_ge(sRGs, 16 * scum[4])
            sy.wait_ge(sRGg, 16 * gcum[4])

        @block.gpsimd
        def _(g):
            for cc in range(8):
                g.dma_start(out=wq_b[:, cc * G3:(cc + 1) * G3],
                            in_=wqkvT[cc * 128:(cc + 1) * 128, :]).then_inc(sW, 16)
            for grp in range(4):
                own = grp < 2
                g.wait_ge(vch, vch_evac(grp * 8 + 7))
                if grp >= 1:
                    g.wait_ge(sRGg, 16 * gcum[grp])
                for sh in [sh for sh in (range(48) if own else range(16, 48)) if sh % 2 == 0]:
                    _emit_regroup(g, sh, grp, ring, qwin, kvo, kvf, sRGg)
            g.wait_ge(sRGg, 16 * gcum[4])
            g.wait_ge(sRGs, 16 * scum[4])

        @block.scalar
        def _(s):
            for i in range(NT):
                p = (i % 2) * 8
                s.wait_ge(vch, vch_cen(i))
                s.activation(sqj[:, :], cen[:, (i % 2) * D:(i % 2 + 1) * D], AF.Square,
                             accum_out=st[:, p + 1:p + 2]).then_inc(sch, 1)
                s.wait_ge(vch, vch_var(i))
                s.activation(st[:, p + 6:p + 7], st[:, p + 5:p + 6], AF.Sqrt,
                             bias=ceps[:, 0:1]).then_inc(sch, 1)

        @block.vector
        def _(v):
            vcnt = [0]

            def vop(inst):
                inst.then_inc(vch, 1)
                vcnt[0] += 1
                v.wait_ge(vch, vcnt[0])

            vop(v.memset(ceps[:], EPS))

            def ln_v(i):
                p = (i % 2) * 8
                v.wait_ge(sX0 if i % 2 == 0 else sX1, 16 * (i // 2 + 1))
                vop(v.reduce_sum(st[:, p + 0:p + 1], xt[:, (i % 2) * D:(i % 2 + 1) * D], axis=AX.X))
                vop(v.tensor_scalar(st[:, p + 2:p + 3], st[:, p + 0:p + 1], 1.0 / D, None, OP.mult))
                vop(v.tensor_scalar(cen[:, (i % 2) * D:(i % 2 + 1) * D],
                                    xt[:, (i % 2) * D:(i % 2 + 1) * D],
                                    st[:, p + 2:p + 3], None, OP.subtract))
                assert vcnt[0] == vch_cen(i), (i, vcnt[0])
                v.wait_ge(sch, sch_sq(i))
                vop(v.tensor_scalar(st[:, p + 5:p + 6], st[:, p + 1:p + 2], 1.0 / D, None, OP.mult))
                assert vcnt[0] == vch_var(i), (i, vcnt[0])
                v.wait_ge(sch, sch_sd(i))
                vop(v.reciprocal(st[:, p + 6:p + 7], st[:, p + 6:p + 7]))
                if i >= 2:
                    v.wait_ge(sTW0 if i % 2 == 0 else sTW1, 16 * (i // 2))
                vop(v.tensor_scalar(lnt[:, (i % 2) * D:(i % 2 + 1) * D],
                                    cen[:, (i % 2) * D:(i % 2 + 1) * D],
                                    st[:, p + 6:p + 7], None, OP.mult))
                assert vcnt[0] == vch_norm(i), (i, vcnt[0])

            def evac(i):
                grp = i // SLOTS
                if grp >= 1:
                    v.wait_ge(sRGg, 16 * gcum[grp])
                    v.wait_ge(sRGs, 16 * scum[grp])
                slot = i % SLOTS
                v.wait_ge(tch, tch_A(i))
                for k, go in enumerate(goffA(i)):
                    vop(v.tensor_copy(ring[:, slot * G3 + go:slot * G3 + go + 512],
                                      psA[:, k * 512:(k + 1) * 512]))
                v.wait_ge(tch, tch_B(i))
                for k, go in enumerate(goffB(i)):
                    vop(v.tensor_copy(ring[:, slot * G3 + go:slot * G3 + go + 512],
                                      psB[:, k * 512:(k + 1) * 512]))
                assert vcnt[0] == vch_evac(i), (i, vcnt[0])

            for i in range(NTC):
                ln_v(i)
            for k in range(NTC):
                ln_v(NTC + k)
                evac(k)
            for k in range(NTC, NT):
                evac(k)

        @block.tensor
        def _(t):
            t.wait_ge(sW, 16 * 8)
            for i in range(NT):
                fmx = fmo if i < NTC else fmf
                if i == 0:
                    t.wait_ge(sFMo, 16 * 8)
                if i == NTC:
                    t.wait_ge(sFMf, 16 * 8)
                tl = i % NTC
                if i >= 1:
                    t.wait_ge(vch, vch_evac(i - 1))
                for ps, goffs in [(psA, goffA(i)), (psB, goffB(i))]:
                    for cc in range(8):
                        statn = fmx[:, cc * T + tl * 128: cc * T + (tl + 1) * 128]
                        for k, go in enumerate(goffs):
                            mm = t.matmul(ps[:, k * 512:(k + 1) * 512], statn,
                                          wq_b[:, cc * G3 + go: cc * G3 + go + 512],
                                          start=(cc == 0), stop=(cc == 7))
                            if cc == 7:
                                mm.then_inc(tch, 1)
    return nc


def _emit_regroup(eng, sh, grp, ring, qwin, kvo, kvf, sem, rowstride=RS):
    own = grp < 2
    g_loc = grp % 2
    src = _ap(ring if isinstance(ring, type(qwin)) else ring[:],
              [[rowstride, 128], [G3, SLOTS], [1, 64]], sh * 64)
    if sh < 16:
        base, sh2 = qwin, sh
    else:
        base, sh2 = (kvo if own else kvf), sh - 16
    dst = _ap(base, [[64, 128], [4 * DF, SLOTS], [1, 64]],
              sh2 * (WQ * DF) + g_loc * 32 * DF)
    eng.dma_start(out=dst, in_=src).then_inc(sem, 16)


def kernel(L0_attn_x, L0_attn_Wqkv, L0_attn_Wproj, L0_ffn_rhs):
    x = np.ascontiguousarray(np.asarray(L0_attn_x, dtype=np.float32))
    wqkvT = np.ascontiguousarray(np.asarray(L0_attn_Wqkv, dtype=np.float32).T)
    wprojT = np.ascontiguousarray(np.asarray(L0_attn_Wproj, dtype=np.float32).T)
    rhsT = np.ascontiguousarray(np.asarray(L0_ffn_rhs, dtype=np.float32).T)

    nc = build_fused()
    in_maps = []
    for c in range(8):
        b, q = c // 4, c % 4
        fq = q ^ 2
        in_maps.append({
            "xq": np.ascontiguousarray(x[b, q * T:(q + 1) * T]),
            "xf": np.ascontiguousarray(x[b, fq * T:(fq + 1) * T]),
            "wqkvT": wqkvT,
            "wprojT": wprojT,
            "rhsT": rhsT,
        })
    res = run_bass_kernel_spmd(nc, in_maps, core_ids=list(range(8)))
    out = np.empty((2, 8192, D), dtype=np.float32)
    for c in range(8):
        b, q = c // 4, c % 4
        out[b, q * T:(q + 1) * T] = np.asarray(res.results[c]["y"])
    return out


def kernel_unfused(L0_attn_x, L0_attn_Wqkv, L0_attn_Wproj, L0_ffn_rhs):
    x = np.ascontiguousarray(np.asarray(L0_attn_x, dtype=np.float32))
    wqkvT = np.ascontiguousarray(np.asarray(L0_attn_Wqkv, dtype=np.float32).T)
    wprojT = np.ascontiguousarray(np.asarray(L0_attn_Wproj, dtype=np.float32).T)
    rhsT = np.ascontiguousarray(np.asarray(L0_ffn_rhs, dtype=np.float32).T)

    nc1 = build_l1()
    in_maps = []
    for c in range(8):
        b, q = c // 4, c % 4
        fq = q ^ 2
        in_maps.append({
            "xq": np.ascontiguousarray(x[b, q * T:(q + 1) * T]),
            "xf": np.ascontiguousarray(x[b, fq * T:(fq + 1) * T]),
            "wqkvT": wqkvT,
        })
    res1 = run_bass_kernel_spmd(nc1, in_maps, core_ids=list(range(8)))

    kvw = []
    for b in range(2):
        r0, r1 = res1.results[4 * b + 0], res1.results[4 * b + 1]
        kvw.append(np.ascontiguousarray(np.concatenate(
            [np.asarray(r0["kvo"]), np.asarray(r1["kvo"]),
             np.asarray(r0["kvf"]), np.asarray(r1["kvf"])], axis=2)))

    nc2 = build_l2()
    in_maps2 = []
    for c in range(8):
        b, q = c // 4, c % 4
        in_maps2.append({
            "kvwin": kvw[b],
            "qwin2": np.asarray(res1.results[c]["qwin"]),
            "xq": np.ascontiguousarray(x[b, q * T:(q + 1) * T]),
            "wprojT": wprojT,
            "rhsT": rhsT,
        })
    res2 = run_bass_kernel_spmd(nc2, in_maps2, core_ids=list(range(8)))
    out = np.empty((2, 8192, D), dtype=np.float32)
    for c in range(8):
        b, q = c // 4, c % 4
        out[b, q * T:(q + 1) * T] = np.asarray(res2.results[c]["y"])
    return out


if __name__ == "__main__":
    pass


def v2_rs(oc):
    return 2 + 8 * oc + 2


def v2_qb(oc):
    return 2 + 8 * (oc + 1)


V2_RSB = 67


def v2_smax(h):
    return 67 + 9 * h + 1


def v2_nb(h):
    return 67 + 9 * h + 2


def v2_recip(h):
    return 67 + 9 * h + 3


def v2_pt(h, k):
    return 67 + 9 * h + 4 + k


def v2_hend(h):
    return 76 + 9 * h


V2_P1END = 211


def v3_nm(tc):
    return 211 + 4 * tc + 1


def v3_norm(tc):
    return 211 + 4 * tc + 3


def v3_end(tc):
    return 211 + 4 * tc + 4


V2_P3END = 275


_V4NM, _V4LN2, _V4CAST = {}, {}, {}
_c = 275 + 1
_V4NM[0] = _c
for _tc in range(NTC):
    _c += 1
    _c += 1
    _V4LN2[_tc] = _c
    if _tc + 1 < NTC:
        _c += 1
        _V4NM[_tc + 1] = _c
    _c += 6
    _V4CAST[_tc] = _c
V2_P4END = _c
assert V2_P4END == 419


def v4_nm(tc):
    return _V4NM[tc]


def v4_ln2(tc):
    return _V4LN2[tc]


def v4_cast(tc):
    return _V4CAST[tc]


def v5_zr(tc):
    return 419 + 9 * tc + 1


def v5_nm(tc):
    return 419 + 9 * tc + 2


def v5_vp(tc):
    return 419 + 9 * tc + 5


def v5_y(tc):
    return 419 + 9 * tc + 9


def s2_sgn(oc):
    return oc + 1


def s2_exp(h):
    return 9 + h


S2_P1END = 24


def s3_sum(tc):
    return 25 + 3 * tc


def s3_sq(tc):
    return 26 + 3 * tc


def s3_sd(tc):
    return 27 + 3 * tc


S2_P3END = 72


def s4_sq(tc):
    return 73 + 5 * tc


def s4_sd(tc):
    return 74 + 5 * tc


def s4_sgn(tc):
    return 75 + 5 * tc


def s4_half(tc):
    return 76 + 5 * tc


def s4_cast(tc):
    return 77 + 5 * tc


S2_P4END = 152


def s5_sq(tc):
    return 153 + 2 * tc


def s5_sd(tc):
    return 154 + 2 * tc


def t2_S(h):
    return 3 + 7 * h


def t2_PT(h, k):
    return 4 + 7 * h + k


def t2_PV(h, fc):
    return 6 + 7 * h + fc


def t2_hend(h):
    return 9 + 7 * h


T2_P1END = 114


def t3(tc):
    return 114 + 2 * (tc + 1)


def t5(tc):
    return 146 + 2 * (tc + 1)


def build_l2(cut=5, n3=NTC, p3mask=15, rep=0):
    nc = bass.Bass("TRN2", target_bir_lowering=False)
    kvwin = nc.dram_tensor("kvwin", [2, H, WIN, DF], BF16, kind="ExternalInput").ap()
    qwin2 = nc.dram_tensor("qwin2", [H, WQ, DF], BF16, kind="ExternalInput").ap()
    xq = nc.dram_tensor("xq", [T, D], F32, kind="ExternalInput").ap()
    wprojT = nc.dram_tensor("wprojT", [D, D], F32, kind="ExternalInput").ap()
    rhsT = nc.dram_tensor("rhsT", [D, D], F32, kind="ExternalInput").ap()
    y = nc.dram_tensor("y", [T, D], F32, kind="ExternalOutput").ap()
    otm = nc.dram_tensor("otm", [T, D], BF16).ap()
    qrt = nc.dram_tensor("qrt", [D, D], BF16).ap()
    qltm = nc.dram_tensor("qltm", [T, D], BF16).ap()
    ident_dram = nc.inline_tensor(np.eye(128, dtype=BFNP), "identc").ap()

    with contextlib.ExitStack() as ES:
        E = ES.enter_context
        wpb = E(nc.sbuf_tensor("wpb", [128, 8 * D], BF16))
        qrs = E(nc.sbuf_tensor("qrs", [128, 8 * D], BF16))
        rsb = E(nc.sbuf_tensor("rsb", [128, D], F32))
        rsT = E(nc.sbuf_tensor("rsT", [128, 8], F32))
        rsrow = E(nc.sbuf_tensor("rsrow", [1, D], F32))
        onesf = E(nc.sbuf_tensor("onesf", [1, 128], F32))
        ident = E(nc.sbuf_tensor("ident", [128, 128], BF16))
        ceps = E(nc.sbuf_tensor("ceps", [128, 1], F32))
        rqt = E(nc.sbuf_tensor("rqt", [128, D], F32))
        sgnb = E(nc.sbuf_tensor("sgnb", [128, D], F32))
        zib = E(nc.sbuf_tensor("zib", [128, D], I32))
        qbb = E(nc.sbuf_tensor("qbb", [128, 2 * D], BF16))
        qts = E(nc.sbuf_tensor("qts", [128, H * 16 * 64], BF16))
        ktb = E(nc.sbuf_tensor("ktb", [128, 2 * 16 * 256], BF16))
        vb = E(nc.sbuf_tensor("vb", [128, 2 * 2 * DF], BF16))
        pb = E(nc.sbuf_tensor("pb", [64, 256], BF16))
        ptb = E(nc.sbuf_tensor("ptb", [128, 128], BF16))
        rsum = E(nc.sbuf_tensor("rsum", [64, 1], F32))
        smax = E(nc.sbuf_tensor("smax", [64, 1], F32))
        nb1 = E(nc.sbuf_tensor("nb1", [64, 1], F32))
        rr = E(nc.sbuf_tensor("rr", [64, 1], F32))
        scr8 = E(nc.sbuf_tensor("scr8", [128, 16 * D], F32))
        ofm = qts
        xt2 = E(nc.sbuf_tensor("xt2", [128, 2 * D], F32))
        st2 = E(nc.sbuf_tensor("st2", [128, 16], F32))
        ln2t = E(nc.sbuf_tensor("ln2t", [128, D], F32))
        lst = E(nc.sbuf_tensor("lst", [128, 16], F32))
        zrt = E(nc.sbuf_tensor("zrt", [128, D], F32))
        nrm = E(nc.sbuf_tensor("nrm", [128, D], F32))
        psS = E(nc.psum_tensor("psS", [64, 256], F32))
        psPT = E(nc.psum_tensor("psPT", [128, 64], BF16))
        psO = E(nc.psum_tensor("psO", [64, DF], F32))
        psP = E(nc.psum_tensor("psP", [128, D], F32))

        ow = scr8[:].bitcast(BF16)
        x1 = scr8

        sRQ0 = E(nc.semaphore("sRQ0"))
        sRQ1 = E(nc.semaphore("sRQ1"))
        sQW0 = E(nc.semaphore("sQW0"))
        sQW1 = E(nc.semaphore("sQW1"))
        sQR = E(nc.semaphore("sQR"))
        sQT = E(nc.semaphore("sQT"))
        sKT0 = E(nc.semaphore("sKT0"))
        sKT1 = E(nc.semaphore("sKT1"))
        sV0 = E(nc.semaphore("sV0"))
        sV1 = E(nc.semaphore("sV1"))
        sW2 = E(nc.semaphore("sW2"))
        sID = E(nc.semaphore("sID"))
        sRR = E(nc.semaphore("sRR"))
        sOG = E(nc.semaphore("sOG"))
        sOF = E(nc.semaphore("sOF"))
        sX20 = E(nc.semaphore("sX20"))
        sX21 = E(nc.semaphore("sX21"))
        sQL0 = E(nc.semaphore("sQL0"))
        sQL1 = E(nc.semaphore("sQL1"))
        sQF = E(nc.semaphore("sQF"))
        sY0 = E(nc.semaphore("sY0"))
        sY1 = E(nc.semaphore("sY1"))
        vch2 = E(nc.semaphore("vch2"))
        sch2 = E(nc.semaphore("sch2"))
        tch2 = E(nc.semaphore("tch2"))
        block = E(nc.Block())

        @block.sync
        def _(sy):
            for oc in range(8):
                if oc >= 1:
                    sy.wait_ge(vch2, 2 + 8 * (oc - 1) + 4)
                    sy.wait_ge(sch2, s2_sgn(oc - 1))
                sy.dma_start(out=rqt[:, :],
                             in_=rhsT[oc * 128:(oc + 1) * 128, :]).then_inc(sRQ0, 16)
                if oc >= 1:
                    j = oc - 1
                    sy.wait_ge(vch2, v2_qb(j))
                    sy.dma_start(out=qrt[j * 128:(j + 1) * 128, :],
                                 in_=qbb[:, (j % 2) * D:(j % 2 + 1) * D]).then_inc(sQW0 if j % 2 == 0 else sQW1, 16)
            sy.wait_ge(vch2, v2_qb(7))
            sy.dma_start(out=qrt[7 * 128:8 * 128, :],
                         in_=qbb[:, D:2 * D]).then_inc(sQW1, 16)
            sy.wait_ge(sQW0, 16 * 4)
            sy.wait_ge(sQW1, 16 * 4)
            for ic in range(8):
                sy.dma_start(out=qrs[:, ic * D:(ic + 1) * D],
                             in_=qrt[:, ic * 128:(ic + 1) * 128], transpose=True).then_inc(sQR, 16)
            if cut < 1:
                sy.wait_ge(sQR, 16 * 8)
                return
            for h in range(H):
                for c in range(16):
                    sy.dma_start(out=qts[:, (h * 16 + c) * 64:(h * 16 + c + 1) * 64],
                                 in_=qwin2[h, :, c * 128:(c + 1) * 128], transpose=True).then_inc(sQT, 16)
            for h in range(H):
                if h >= 2:
                    sy.wait_ge(tch2, t2_S(h - 2))
                par = h % 2
                for c in range(16):
                    sy.dma_start(out=ktb[:, par * 4096 + c * 256:par * 4096 + (c + 1) * 256],
                                 in_=kvwin[0, h, :, c * 128:(c + 1) * 128], transpose=True).then_inc(
                        sKT0 if par == 0 else sKT1, 16)
            if cut < 2:
                return
            sy.wait_ge(sOG, 16 * 16)
            sy.wait_ge(tch2, t2_S(H - 1))
            for cc in range(8):
                sy.dma_start(out=ofm[:, cc * T:(cc + 1) * T],
                             in_=otm[:, cc * 128:(cc + 1) * 128], transpose=True).then_inc(sOF, 16)
            if cut < 3:
                sy.wait_ge(sOF, 16 * 8)
                return
            for tc in (range(n3) if p3mask & 8 else []):
                if tc >= 2:
                    sy.wait_ge(vch2, v3_end(tc - 2))
                sy.dma_start(out=xt2[:, (tc % 2) * D:(tc % 2 + 1) * D],
                             in_=xq[tc * 128:(tc + 1) * 128, :]).then_inc(sX20 if tc % 2 == 0 else sX21, 16)
            if cut < 4:
                sy.wait_ge(sOF, 16 * 8)
                return
            for tc in range(NTC):
                sy.wait_ge(vch2, v4_qb(tc))
                sy.dma_start(out=qltm[tc * 128:(tc + 1) * 128, :],
                             in_=qbb[:, (tc % 2) * D:(tc % 2 + 1) * D]).then_inc(sQL0 if tc % 2 == 0 else sQL1, 16)
            sy.wait_ge(sQL0, 16 * 8)
            sy.wait_ge(sQL1, 16 * 8)
            sy.wait_ge(tch2, t3(NTC - 1))
            for cc in range(8):
                sy.dma_start(out=ofm[:, cc * T:(cc + 1) * T],
                             in_=qltm[:, cc * 128:(cc + 1) * 128], transpose=True).then_inc(sQF, 16)
            if cut < 5:
                sy.wait_ge(sQF, 16 * 8)
                return
            for tc in range(NTC):
                sy.wait_ge(vch2, v5_y(tc))
                sy.dma_start(out=y[tc * 128:(tc + 1) * 128, :],
                             in_=nrm[:, :]).then_inc(sY0, 16)
            sy.wait_ge(sY0, 16 * 16)

        @block.gpsimd
        def _(g):
            for cc in range(8):
                g.dma_start(out=wpb[:, cc * D:(cc + 1) * D],
                            in_=wprojT[cc * 128:(cc + 1) * 128, :]).then_inc(sW2, 16)
            g.dma_start(out=ident[:], in_=ident_dram[:]).then_inc(sID, 16)
            for oc in range(8):
                g.wait_ge(vch2, v2_rs(oc))
                src = _ap(rsT[:], [[8, 128], [1, 1]], oc)
                dst = _ap(rsrow[:], [[1024, 1], [1, 128]], oc * 128)
                g.dma_start(out=dst, in_=src).then_inc(sRR, 16)
            if cut < 1:
                return
            for h in range(H):
                if h >= 2:
                    g.wait_ge(tch2, t2_PV(h - 2, 3))
                par = h % 2
                for wc in range(2):
                    g.dma_start(out=vb[:, par * 2 * DF + wc * DF:par * 2 * DF + (wc + 1) * DF],
                                in_=kvwin[1, h, wc * 128:(wc + 1) * 128, :]).then_inc(
                        sV0 if par == 0 else sV1, 16)
            if cut < 2:
                return
            OWF = H * DF
            for h in range(H):
                g.wait_ge(vch2, v2_hend(h))
                src = _ap(ow[:], [[OWF, 64], [64, 32], [1, 64]], h * DF)
                dst = _ap(otm[:], [[32 * D, 64], [D, 32], [1, 64]], h * 64)
                g.dma_start(out=dst, in_=src).then_inc(sOG, 16)

        @block.scalar
        def _(s):
            scnt = [0]

            def sop(inst):
                inst.then_inc(sch2, 1)
                scnt[0] += 1
                s.wait_ge(sch2, scnt[0])

            for oc in range(8):
                s.wait_ge(sRQ0, 16 * (oc + 1))
                if oc >= 1:
                    s.wait_ge(vch2, 2 + 8 * (oc - 1) + 5)
                sop(s.activation(sgnb[:, :], rqt[:, :], AF.Sign))
                assert scnt[0] == s2_sgn(oc)
            if cut < 1:
                return
            for h in range(H):
                s.wait_ge(tch2, t2_S(h))
                s.wait_ge(vch2, v2_nb(h))
                if h >= 1:
                    s.wait_ge(tch2, t2_PT(h - 1, 1))
                    s.wait_ge(vch2, v2_recip(h - 1))
                sop(s.activation(pb[:, :], psS[:, :], AF.Exp, scale=0.125, bias=nb1[:, :],
                                 accum_out=rsum[:, :]))
                assert scnt[0] == s2_exp(h)
            if cut < 3:
                return
            for tc in (range(n3) if p3mask & 2 else []):
                s.wait_ge(vch2, v3_cen(tc))
                sop(s.activation(ln2t[:, :], zrt[:, :], AF.Square, accum_out=st2[:, 1:2]))
                s.wait_ge(vch2, v3_var(tc))
                sop(s.activation(st2[:, 6:7], st2[:, 5:6], AF.Sqrt, bias=ceps[:, 0:1]))
                assert scnt[0] == s3_sd(tc)
            if cut < 4:
                return
            for tc in range(NTC):
                s.wait_ge(vch2, v4_cen(tc))
                sop(s.activation(zrt[:, :], ln2t[:, :], AF.Square,
                                 accum_out=st2[:, 9:10]))
                s.wait_ge(vch2, v4_var(tc))
                sop(s.activation(st2[:, 14:15], st2[:, 13:14], AF.Sqrt, bias=ceps[:, 0:1]))
                s.wait_ge(vch2, v4_ln2(tc))
                if tc >= 1:
                    s.wait_ge(vch2, 323 + 15 * (tc - 1) + 10)
                else:
                    s.wait_ge(vch2, 2 + 8 * 7 + 5)
                sop(s.activation(sgnb[:, :], ln2t[:, :], AF.Sign))
                assert scnt[0] == s4_sgn(tc)
            if cut < 5:
                return
            s.wait_ge(sQL0, 16 * 8)
            s.wait_ge(sQL1, 16 * 8)
            for tc in range(NTC):
                s.wait_ge(vch2, v5_cen(tc))
                sop(s.activation(qbb[:, 0:D], ln2t[:, :], AF.Square, accum_out=st2[:, 9:10]))
                s.wait_ge(vch2, v5_vp(tc))
                sop(s.activation(st2[:, 14:15], st2[:, 13:14], AF.Sqrt, bias=ceps[:, 0:1]))
                assert scnt[0] == s5_sd(tc)

        @block.tensor
        def _(t):
            tcnt = [0]

            def top(inst):
                inst.then_inc(tch2, 1)
                tcnt[0] += 1

            t.wait_ge(vch2, 2)
            t.wait_ge(sRR, 16 * 8)
            top(t.matmul(psP[:, 0:512], onesf[:, :], rsrow[:, 0:512], start=True, stop=True))
            top(t.matmul(psP[:, 512:1024], onesf[:, :], rsrow[:, 512:1024], start=True, stop=True))
            if cut < 1:
                return
            for h in range(H):
                par = h % 2
                if h == 0:
                    t.wait_ge(sQT, 16 * 256)
                t.wait_ge(sKT0 if par == 0 else sKT1, 16 * 16 * (h // 2 + 1))
                if h >= 1:
                    t.wait_ge(sch2, s2_exp(h - 1))
                for c in range(16):
                    mm = t.matmul(psS[:, :], qts[:, (h * 16 + c) * 64:(h * 16 + c + 1) * 64],
                                  ktb[:, par * 4096 + c * 256:par * 4096 + (c + 1) * 256],
                                  start=(c == 0), stop=(c == 15))
                    if c == 15:
                        top(mm)
                assert tcnt[0] == t2_S(h)
                t.wait_ge(sch2, s2_exp(h))
                t.wait_ge(sID, 16)
                if h >= 1:
                    t.wait_ge(vch2, v2_pt(h - 1, 1))
                top(t.transpose(psPT[:, :], pb[:, 0:128], ident[0:64, 0:64]))
                t.wait_ge(vch2, v2_pt(h, 0))
                top(t.transpose(psPT[:, :], pb[:, 128:256], ident[0:64, 0:64]))
                t.wait_ge(sV0 if par == 0 else sV1, 16 * 2 * (h // 2 + 1))
                t.wait_ge(vch2, v2_pt(h, 1))
                if h >= 1:
                    t.wait_ge(vch2, v2_hend(h - 1))
                for fc in range(4):
                    for wc in range(2):
                        mm = t.matmul(psO[:, fc * 512:(fc + 1) * 512],
                                      ptb[:, wc * 64:(wc + 1) * 64],
                                      vb[:, par * 2 * DF + wc * DF + fc * 512:par * 2 * DF + wc * DF + fc * 512 + 512],
                                      start=(wc == 0), stop=(wc == 1))
                        if wc == 1:
                            top(mm)
                assert tcnt[0] == t2_hend(h)
            if cut < 3:
                return
            t.wait_ge(sOF, 16 * 8)
            t.wait_ge(sW2, 16 * 8)
            t.wait_ge(vch2, V2_RSB)
            for tc in (range(n3) if p3mask & 1 else []):
                if tc >= 1:
                    t.wait_ge(vch2, v3_norm(tc - 1))
                    t.wait_ge(sch2, s3_sq(tc - 1))
                for nc2 in range(2):
                    for cc in range(8):
                        mm = t.matmul(psP[:, nc2 * 512:(nc2 + 1) * 512],
                                      ofm[:, cc * T + tc * 128:cc * T + (tc + 1) * 128],
                                      wpb[:, cc * D + nc2 * 512:cc * D + nc2 * 512 + 512],
                                      start=(cc == 0), stop=(cc == 7))
                        if cc == 7:
                            top(mm)
                assert tcnt[0] == t3(tc)
            if cut < 5:
                return
            t.wait_ge(sQF, 16 * 8)
            t.wait_ge(sQR, 16 * 8)
            t.wait_ge(vch2, v3_norm(NTC - 1))
            t.wait_ge(sch2, s3_sq(NTC - 1))
            for tc in range(NTC):
                if tc >= 1:
                    t.wait_ge(vch2, v5_zr(tc - 1))
                    t.wait_ge(sch2, s5_sq(tc - 1))
                for nc2 in range(2):
                    for cc in range(8):
                        mm = t.matmul(psP[:, nc2 * 512:(nc2 + 1) * 512],
                                      ofm[:, cc * T + tc * 128:cc * T + (tc + 1) * 128],
                                      qrs[:, cc * D + nc2 * 512:cc * D + nc2 * 512 + 512],
                                      start=(cc == 0), stop=(cc == 7))
                        if cc == 7:
                            top(mm)
                assert tcnt[0] == t5(tc)

        @block.vector
        def _(v):
            vcnt = [0]

            def vop(inst):
                inst.then_inc(vch2, 1)
                vcnt[0] += 1
                v.wait_ge(vch2, vcnt[0])

            vop(v.memset(onesf[:], 1.0))
            vop(v.memset(ceps[:], EPS))
            for oc in range(8):
                par = oc % 2
                v.wait_ge(sRQ0, 16 * (oc + 1))
                vop(v.reduce_max(st2[:, 0:1], rqt[:, :], axis=AX.X,
                                 apply_absolute_value=True))
                vop(v.tensor_scalar(rsT[:, oc:oc + 1], st2[:, 0:1], 1.0 / 127.0, None, OP.mult))
                assert vcnt[0] == v2_rs(oc)
                vop(v.reciprocal(st2[:, 1:2], rsT[:, oc:oc + 1]))
                vop(v.tensor_scalar(ln2t[:, :], rqt[:, :],
                                    st2[:, 1:2], None, OP.mult))
                v.wait_ge(sch2, s2_sgn(oc))
                vop(v.scalar_tensor_tensor(ln2t[:, :], sgnb[:, :], 0.5, ln2t[:, :], OP.mult, OP.add))
                vop(v.tensor_scalar(ln2t[:, :], ln2t[:, :], 127.49, -127.49, OP.min, OP.max))
                vop(v.tensor_copy(zib[:, :], ln2t[:, :]))
                if oc >= 2:
                    v.wait_ge(sQW0 if par == 0 else sQW1, 16 * (oc // 2))
                vop(v.tensor_copy(qbb[:, par * D:(par + 1) * D], zib[:, :]))
                assert vcnt[0] == v2_qb(oc)
            v.wait_ge(tch2, 2)
            vop(v.tensor_copy(rsb[:, :], psP[:, :]))
            assert vcnt[0] == V2_RSB
            if cut < 1:
                return
            for h in range(H):
                v.wait_ge(tch2, t2_S(h))
                if h >= 1:
                    v.wait_ge(sch2, s2_exp(h - 1))
                vop(v.reduce_max(smax[:, :], psS[:, :], axis=AX.X))
                assert vcnt[0] == v2_smax(h)
                vop(v.tensor_scalar(nb1[:, :], smax[:, :], -0.125, None, OP.mult))
                v.wait_ge(sch2, s2_exp(h))
                vop(v.reciprocal(rr[:, :], rsum[:, :]))
                assert vcnt[0] == v2_recip(h)
                for k in range(2):
                    v.wait_ge(tch2, t2_PT(h, k))
                    vop(v.tensor_copy(ptb[:, k * 64:(k + 1) * 64], psPT[:, :]))
                v.wait_ge(tch2, t2_PV(h, 3))
                for fc in range(4):
                    vop(v.tensor_scalar(ow[0:64, h * DF + fc * 512:h * DF + fc * 512 + 512],
                                        psO[:, fc * 512:(fc + 1) * 512],
                                        rr[:, 0:1], None, OP.mult))
                assert vcnt[0] == v2_hend(h)
            if cut < 3:
                return
            for tc in (range(n3) if p3mask & 4 else []):
                if tc == 0:
                    v.wait_ge(sOG, 16 * 16)
                v.wait_ge(tch2, t3(tc))
                vop(v.reduce_sum(st2[:, 0:1], psP[:, :], axis=AX.X))
                vop(v.tensor_scalar(st2[:, 2:3], st2[:, 0:1], 1.0 / D, None, OP.mult))
                if tc >= 1:
                    v.wait_ge(sch2, s3_sq(tc - 1))
                vop(v.tensor_scalar(zrt[:, :], psP[:, :], st2[:, 2:3], None, OP.subtract))
                assert vcnt[0] == v3_cen(tc)
                v.wait_ge(sch2, s3_sq(tc))
                vop(v.tensor_scalar(st2[:, 5:6], st2[:, 1:2], 1.0 / D, None, OP.mult))
                assert vcnt[0] == v3_var(tc)
                v.wait_ge(sch2, s3_sd(tc))
                vop(v.reciprocal(st2[:, 6:7], st2[:, 6:7]))
                vop(v.tensor_scalar(nrm[:, :], zrt[:, :], st2[:, 6:7], None, OP.mult))
                assert vcnt[0] == v3_norm(tc)
                par = tc % 2
                v.wait_ge(sX20 if par == 0 else sX21, 16 * (tc // 2 + 1))
                vop(v.tensor_tensor(x1[:, tc * D:(tc + 1) * D], nrm[:, :], xt2[:, par * D:(par + 1) * D],
                                    OP.add))
                assert vcnt[0] == v3_end(tc)
            if cut < 4:
                return
            for tc in range(NTC):
                vop(v.reduce_sum(st2[:, 8:9], x1[:, tc * D:(tc + 1) * D], axis=AX.X))
                vop(v.tensor_scalar(st2[:, 10:11], st2[:, 8:9], 1.0 / D, None, OP.mult))
                if tc >= 1:
                    v.wait_ge(sch2, s4_sq(tc - 1))
                vop(v.tensor_scalar(ln2t[:, :], x1[:, tc * D:(tc + 1) * D],
                                    st2[:, 10:11], None, OP.subtract))
                assert vcnt[0] == v4_cen(tc)
                v.wait_ge(sch2, s4_sq(tc))
                vop(v.tensor_scalar(st2[:, 13:14], st2[:, 9:10], 1.0 / D, None, OP.mult))
                assert vcnt[0] == v4_var(tc)
                v.wait_ge(sch2, s4_sd(tc))
                vop(v.reciprocal(st2[:, 14:15], st2[:, 14:15]))
                vop(v.tensor_scalar(ln2t[:, :], ln2t[:, :], st2[:, 14:15], None, OP.mult))
                assert vcnt[0] == v4_ln2(tc)
                vop(v.reduce_max(st2[:, 0:1], ln2t[:, :], axis=AX.X, apply_absolute_value=True))
                vop(v.reciprocal(st2[:, 1:2], st2[:, 0:1]))
                vop(v.tensor_scalar(st2[:, 2:3], st2[:, 1:2], 127.0, None, OP.mult))
                vop(v.tensor_scalar(lst[:, tc:tc + 1], st2[:, 0:1], 1.0 / 127.0, None, OP.mult))
                v.wait_ge(sch2, s4_sgn(tc))
                vop(v.tensor_scalar(ln2t[:, :], ln2t[:, :], st2[:, 2:3], None, OP.mult))
                vop(v.scalar_tensor_tensor(ln2t[:, :], sgnb[:, :], 0.5, ln2t[:, :], OP.mult, OP.add))
                vop(v.tensor_scalar(ln2t[:, :], ln2t[:, :], 127.49, -127.49, OP.min, OP.max))
                vop(v.tensor_copy(zib[:, :], ln2t[:, :]))
                par = tc % 2
                if tc >= 2:
                    v.wait_ge(sQL0 if par == 0 else sQL1, 16 * (tc // 2))
                vop(v.tensor_copy(qbb[:, par * D:(par + 1) * D], zib[:, :]))
                assert vcnt[0] == v4_qb(tc)
            if cut < 5:
                return
            for tc in range(NTC):
                if tc == 0:
                    v.wait_ge(sch2, S2_P4END)
                else:
                    v.wait_ge(sch2, s5_sq(tc - 1))
                v.wait_ge(tch2, t5(tc))
                vop(v.tensor_tensor(zrt[:, :], psP[:, :], rsb[:, :], OP.mult))
                assert vcnt[0] == v5_zr(tc)
                vop(v.reduce_sum(st2[:, 8:9], zrt[:, :], axis=AX.X))
                vop(v.tensor_scalar(st2[:, 10:11], st2[:, 8:9], 1.0 / D, None, OP.mult))
                vop(v.tensor_scalar(ln2t[:, :], zrt[:, :], st2[:, 10:11], None, OP.subtract))
                assert vcnt[0] == v5_cen(tc)
                v.wait_ge(sch2, s5_sq(tc))
                vop(v.tensor_scalar(st2[:, 13:14], st2[:, 9:10], 1.0 / D, None, OP.mult))
                vop(v.tensor_tensor(st2[:, 4:5], lst[:, tc:tc + 1], lst[:, tc:tc + 1], OP.mult))
                vop(v.tensor_tensor(st2[:, 13:14], st2[:, 13:14], st2[:, 4:5], OP.mult))
                assert vcnt[0] == v5_vp(tc)
                v.wait_ge(sch2, s5_sd(tc))
                vop(v.reciprocal(st2[:, 14:15], st2[:, 14:15]))
                vop(v.tensor_tensor(st2[:, 5:6], lst[:, tc:tc + 1], st2[:, 14:15], OP.mult))
                if tc >= 1:
                    v.wait_ge(sY0, 16 * tc)
                vop(v.tensor_scalar(nrm[:, :], ln2t[:, :], st2[:, 5:6], None, OP.mult))
                vop(v.tensor_tensor(nrm[:, :], nrm[:, :],
                                    x1[:, tc * D:(tc + 1) * D], OP.add))
                assert vcnt[0] == v5_y(tc)
    return nc


VT1 = 257
SCH1 = 3 * NT
TCH1 = 160
PRB = 32768
PAW = 24576
RSV = 8 * 1024


def tch_fm(k):
    return 6 * k + 4


def tch_v(k):
    return 6 * k + 6


def tch_ffm(k):
    return 96 + 4 * (k + 1)


def tch_fv(k):
    return 128 + 2 * (k + 1)


def vch_nm(i):
    return 3 * i + 2 if i < NTC else 50 + 9 * (i - NTC)


def vch_norm2(i):
    return 3 * i + 4 if i < NTC else 52 + 9 * (i - NTC)


def vch_fmevac(k):
    return 56 + 9 * k


def vch_evac2(k):
    return 58 + 9 * k


def vch_ffm_e(k):
    return 193 + 4 * (k + 1)


def vch_fv_e(k):
    return 225 + 2 * (k + 1)


def sch_sum(i):
    return 3 * i + 1


def sch_sq2(i):
    return 3 * i + 2


def sch_sd2(i):
    return 3 * i + 3


def build_fused():
    nc = bass.Bass("TRN2", target_bir_lowering=False)
    xq = nc.dram_tensor("xq", [T, D], F32, kind="ExternalInput").ap()
    xf = nc.dram_tensor("xf", [T, D], F32, kind="ExternalInput").ap()
    wqkvT = nc.dram_tensor("wqkvT", [D, G3], F32, kind="ExternalInput").ap()
    wprojT = nc.dram_tensor("wprojT", [D, D], F32, kind="ExternalInput").ap()
    rhsT = nc.dram_tensor("rhsT", [D, D], F32, kind="ExternalInput").ap()
    y = nc.dram_tensor("y", [T, D], F32, kind="ExternalOutput").ap()
    kvo = nc.dram_tensor("kvo", [H, WQ, DF], BF16).ap()
    kvf = nc.dram_tensor("kvf", [H, WQ, DF], BF16).ap()
    kfmP = nc.dram_tensor("kfmP", [2, 8, 128, T], BF16).ap()
    kvsh = nc.dram_tensor("kvsh", [H, WIN, DF], BF16, addr_space="Shared").ap()
    kfmS = nc.dram_tensor("kfmS", [8, 128, 4 * T], BF16, addr_space="Shared").ap()
    ln_tm = nc.dram_tensor("ln_tm", [2, T, D], BF16).ap()
    otm = nc.dram_tensor("otm", [T, D], BF16).ap()
    qrt = nc.dram_tensor("qrt", [D, D], BF16).ap()
    qltm = nc.dram_tensor("qltm", [T, D], BF16).ap()
    cbin = nc.dram_tensor("cbin", [1, 64], BF16).ap()
    cbout = nc.dram_tensor("cbout", [4, 64], BF16).ap()
    ident_dram = nc.inline_tensor(np.eye(128, dtype=BFNP), "identc").ap()

    with contextlib.ExitStack() as ES:
        E = ES.enter_context
        PA = E(nc.sbuf_tensor("PA", [128, PAW], BF16))
        PB = E(nc.sbuf_tensor("PB", [128, 16384], BF16))
        PC = E(nc.sbuf_tensor("PC", [128, 16384], BF16))
        PR = E(nc.sbuf_tensor("PR", [128, 16384], F32))
        PD = E(nc.sbuf_tensor("PD", [128, 7216], F32))
        identT = E(nc.sbuf_tensor("identT", [128, 128], BF16))
        psAT = E(nc.psum_tensor("psAT", [128, 2048], F32))
        psBT = E(nc.psum_tensor("psBT", [128, 2048], F32))

        wq_b = PA[:]
        fmo = PB[:]
        fmf = PB[:]
        PRb = PR[:].bitcast(BF16)
        ringv = PRb
        kfmb = PRb
        qfm = PC[:, 0:16384]
        xt = PD[:, 0:2048]
        cen = PD[:, 2048:4096]
        sqj = PD[:, 4096:5120]
        lnt = PD[:, 5120:6144].bitcast(BF16)
        st = PD[:, 6200:6216]
        ceps = PD[:, 6216:6217]
        psA = psAT[:]
        psB = psBT[:, 0:1024]

        kfr = PA[:, 0:8192]
        vb = PA[:, 8192:16384]
        owb = PA[0:64, 16384:20480]
        wpb = PB[:, 0:8192]
        qrs = PB[:, 8192:16384]
        qbb = PA[:, 20480:22528]
        rsb = PA[:, 22528:24576].bitcast(F32)
        pb = PR[0:64, 0:128].bitcast(BF16)
        ptb = PR[:, 128:192].bitcast(BF16)
        rsrow = PR[0:1, 192:1216]
        onesf_ap = PR[0:1, 1216:1344]
        ofm = PC[:, 0:16384]
        x1 = PR[:]
        rqt = PD[:, 0:1024]
        xsum = PD[:, 0:16]
        sgnb = PD[:, 1024:2048]
        zib = PD[:, 1024:2048].bitcast(I32)
        ln2t = PD[:, 2048:3072]
        zrt = PD[:, 3072:4096]
        nrm = PD[:, 4096:5120]
        smax = PD[0:64, 5120:5121]
        nb1 = PD[0:64, 5121:5122]
        rsum = PD[0:64, 5122:5123]
        rr = PD[0:64, 5123:5124]
        xt2 = PD[:, 5124:7172]
        st2 = PD[:, 7172:7188]
        lst = PD[:, 7188:7204]
        rsT = PD[:, 7204:7212]
        ceps2 = PD[:, 7212:7213]
        ident = identT[:]
        psP = psAT[:, 0:1024]
        psS = psAT[0:64, 1024:1280]
        psPT = psAT[:, 1536:1568].bitcast(BF16)
        psO = psBT[0:64, 0:2048]

        sX0 = E(nc.semaphore("sX0")); sX1 = E(nc.semaphore("sX1"))
        sW = E(nc.semaphore("sW"))
        sTW0 = E(nc.semaphore("sTW0")); sTW1 = E(nc.semaphore("sTW1"))
        sFMo = E(nc.semaphore("sFMo")); sFMf = E(nc.semaphore("sFMf"))
        sRGg = E(nc.semaphore("sRGg")); sRGs = E(nc.semaphore("sRGs"))
        sKW0 = E(nc.semaphore("sKW0")); sKW1 = E(nc.semaphore("sKW1"))
        vch = E(nc.semaphore("vch")); sch = E(nc.semaphore("sch")); tch = E(nc.semaphore("tch"))
        sRL = E(nc.semaphore("sRL")); sBAR = E(nc.semaphore("sBAR")); sCB = E(nc.semaphore("sCB"))
        sRL2 = E(nc.semaphore("sRL2")); sRL3 = E(nc.semaphore("sRL3")); sRL4 = E(nc.semaphore("sRL4"))
        sRG3 = E(nc.semaphore("sRG3")); sKH0 = E(nc.semaphore("sKH0")); sKH1 = E(nc.semaphore("sKH1"))
        gch = E(nc.semaphore("gch"))
        sJ1 = E(nc.semaphore("sJ1")); sJ2 = E(nc.semaphore("sJ2"))
        sRQ0 = E(nc.semaphore("sRQ0"))
        sQW0 = E(nc.semaphore("sQW0")); sQW1 = E(nc.semaphore("sQW1"))
        sQR = E(nc.semaphore("sQR"))
        sKF0 = E(nc.semaphore("sKF0")); sKF1 = E(nc.semaphore("sKF1"))
        sV0 = E(nc.semaphore("sV0")); sV1 = E(nc.semaphore("sV1"))
        sW2 = E(nc.semaphore("sW2")); sID = E(nc.semaphore("sID")); sRR = E(nc.semaphore("sRR"))
        sOG = E(nc.semaphore("sOG")); sOF = E(nc.semaphore("sOF"))
        sX20 = E(nc.semaphore("sX20")); sX21 = E(nc.semaphore("sX21"))
        sQL0 = E(nc.semaphore("sQL0")); sQL1 = E(nc.semaphore("sQL1"))
        sQF = E(nc.semaphore("sQF")); sY0 = E(nc.semaphore("sY0"))
        vch2 = E(nc.semaphore("vch2")); sch2 = E(nc.semaphore("sch2")); tch2 = E(nc.semaphore("tch2"))
        block = E(nc.Block())

        def vregroup(eng, h, grp, sem):
            own = grp < 2
            g_loc = grp % 2
            src = _ap(ringv, [[PRB, 128], [1024, 8], [1, 64]],
                      (0 if own else 12288) + h * 64)
            dst = _ap(kvo if own else kvf, [[64, 128], [4 * DF, 8], [1, 64]],
                      h * (WQ * DF) + g_loc * 32 * DF)
            eng.dma_start(out=dst, in_=src).then_inc(sem, 16)

        def gate_l2(eng):
            eng.wait_ge(vch, VT1); eng.wait_ge(sch, SCH1); eng.wait_ge(tch, TCH1)
            eng.wait_ge(sRGg, 16 * 32); eng.wait_ge(sRGs, 16 * 24); eng.wait_ge(sRG3, 16 * 8)
            eng.wait_ge(sKW0, 16 * 8); eng.wait_ge(sKW1, 16 * 8)
            eng.wait_ge(sTW0, 16 * 16); eng.wait_ge(sTW1, 16 * 16)
            eng.wait_ge(sFMo, 128); eng.wait_ge(sFMf, 128)
            eng.wait_ge(sX0, 256); eng.wait_ge(sX1, 256); eng.wait_ge(sW, 128)

        @block.sync
        def _(sy):
            def load(i):
                src = xq if i < NTC else xf
                tl = i % NTC
                if i >= 2:
                    sy.wait_ge(vch, vch_norm2(i - 2))
                    sy.wait_ge(sch, sch_sq2(i - 2))
                sy.dma_start(out=xt[:, (i % 2) * D:(i % 2 + 1) * D],
                             in_=src[tl * 128:(tl + 1) * 128, :]).then_inc(sX0 if i % 2 == 0 else sX1, 16)

            def write(j):
                sy.wait_ge(vch, vch_norm2(j))
                sy.dma_start(out=ln_tm[j // NTC, (j % NTC) * 128:(j % NTC + 1) * 128, :],
                             in_=lnt[:, (j % 2) * D:(j % 2 + 1) * D]).then_inc(sTW0 if j % 2 == 0 else sTW1, 16)

            for i in range(NTC):
                load(i)
                if i >= 1:
                    write(i - 1)
            write(NTC - 1)
            sy.wait_ge(sTW0, 16 * 8)
            sy.wait_ge(sTW1, 16 * 8)
            for k in range(8):
                sy.dma_start(out=fmo[:, k * T:(k + 1) * T],
                             in_=ln_tm[0, :, k * 128:(k + 1) * 128], transpose=True).then_inc(sFMo, 16)

            def rg_grp(grp, gate):
                sy.wait_ge(vch, gate)
                if grp >= 1:
                    sy.wait_ge(sRGs, 16 * 8 * grp)
                for h in range(1, 16, 2):
                    vregroup(sy, h, grp, sRGs)

            def kfm_write(own_idx, kp, gate):
                sy.wait_ge(vch, gate)
                sy.dma_start(out=kfmP[own_idx, kp],
                             in_=kfmb[:, 8192 + (kp % 2) * 2048:8192 + (kp % 2) * 2048 + 2048]
                             ).then_inc(sKW0 if kp % 2 == 0 else sKW1, 16)

            for i in range(NTC, NT):
                load(i)
                if i >= NTC + 1:
                    write(i - 1)
                if i == NTC + 9:
                    rg_grp(0, vch_evac2(7))
                if i >= NTC + 8:
                    kp = i - NTC - 8
                    kfm_write(0, kp, vch_fmevac(8 + kp))
            write(NT - 1)
            sy.wait_ge(sTW0, 16 * 16)
            sy.wait_ge(sTW1, 16 * 16)
            sy.wait_ge(tch, tch_v(15))
            for k in range(8):
                sy.dma_start(out=fmf[:, k * T:(k + 1) * T],
                             in_=ln_tm[1, :, k * 128:(k + 1) * 128], transpose=True).then_inc(sFMf, 16)
            rg_grp(1, vch_evac2(15))
            for kp in range(8):
                kfm_write(1, kp, vch_ffm_e(kp))
            rg_grp(2, vch_fv_e(7))
            sy.wait_ge(sRGg, 16 * 32)
            sy.wait_ge(sRGs, 16 * 24)
            sy.wait_ge(sRG3, 16 * 8)
            for arm in nc.switch_core_id(engines=[sy], n=128):
                qq = arm.logical % 4
                fqq = qq ^ 2
                sy.dma_start(out=_ap(kvsh, [[WIN * DF, 8], [1, WQ * DF]], fqq * WQ * DF),
                             in_=kvf[0:8]).then_inc(sRL4, 16)
            gate_l2(sy)
            for oc in range(8):
                if oc >= 1:
                    sy.wait_ge(vch2, 2 + 8 * (oc - 1) + 4)
                    sy.wait_ge(sch2, s2_sgn(oc - 1))
                sy.dma_start(out=rqt[:, :],
                             in_=rhsT[oc * 128:(oc + 1) * 128, :]).then_inc(sRQ0, 16)
                if oc >= 1:
                    j = oc - 1
                    sy.wait_ge(vch2, v2_qb(j))
                    sy.dma_start(out=qrt[j * 128:(j + 1) * 128, :],
                                 in_=qbb[:, (j % 2) * D:(j % 2 + 1) * D]).then_inc(sQW0 if j % 2 == 0 else sQW1, 16)
            sy.wait_ge(vch2, v2_qb(7))
            sy.dma_start(out=qrt[7 * 128:8 * 128, :],
                         in_=qbb[:, D:2 * D]).then_inc(sQW1, 16)
            sy.wait_ge(sQW0, 16 * 4)
            sy.wait_ge(sQW1, 16 * 4)
            for ic in range(8):
                sy.dma_start(out=qrs[:, ic * D:(ic + 1) * D],
                             in_=qrt[:, ic * 128:(ic + 1) * 128], transpose=True).then_inc(sQR, 16)
            sy.wait_ge(sBAR, 1)
            for p in range(8):
                if p >= 1:
                    sy.wait_ge(tch2, t2_S(2 * p - 1))
                sy.dma_start(out=kfr[:, 0:4096],
                             in_=kfmS[p][:, 0:4096]).then_inc(sKF0 if p % 2 == 0 else sKF1, 16)
            sy.wait_ge(sOG, 16 * 16)
            sy.wait_ge(tch2, T2_P1END)
            for cc in range(8):
                sy.dma_start(out=ofm[:, cc * T:(cc + 1) * T],
                             in_=otm[:, cc * 128:(cc + 1) * 128], transpose=True).then_inc(sOF, 16)
            for tc in range(NTC):
                if tc >= 2:
                    sy.wait_ge(vch2, v3_end(tc - 2))
                sy.dma_start(out=xt2[:, (tc % 2) * D:(tc % 2 + 1) * D],
                             in_=xq[tc * 128:(tc + 1) * 128, :]).then_inc(sX20 if tc % 2 == 0 else sX21, 16)
            for tc in range(NTC):
                sy.wait_ge(gch, tc + 1)
                sy.dma_start(out=qltm[tc * 128:(tc + 1) * 128, :],
                             in_=qbb[:, (tc % 2) * D:(tc % 2 + 1) * D]).then_inc(sQL0 if tc % 2 == 0 else sQL1, 16)
            sy.wait_ge(sQL0, 16 * 8)
            sy.wait_ge(sQL1, 16 * 8)
            sy.wait_ge(tch2, t3(NTC - 1))
            for cc in range(8):
                sy.dma_start(out=ofm[:, cc * T:(cc + 1) * T],
                             in_=qltm[:, cc * 128:(cc + 1) * 128], transpose=True).then_inc(sQF, 16)
            for tc in range(NTC):
                sy.wait_ge(vch2, v5_y(tc))
                sy.dma_start(out=y[tc * 128:(tc + 1) * 128, :],
                             in_=nrm[:, :]).then_inc(sY0, 16)
            sy.wait_ge(sY0, 16 * 16)

        @block.gpsimd
        def _(g):
            for cc in range(8):
                g.dma_start(out=wq_b[:, cc * G3:(cc + 1) * G3],
                            in_=wqkvT[cc * 128:(cc + 1) * 128, :]).then_inc(sW, 16)
            for grp in range(3):
                gate = vch_evac2(grp * 8 + 7) if grp < 2 else vch_fv_e(grp * 8 - 9)
                g.wait_ge(vch, gate)
                if grp >= 1:
                    g.wait_ge(sRGg, 16 * 8 * grp)
                for h in range(0, 16, 2):
                    vregroup(g, h, grp, sRGg)
            g.wait_ge(sKW0, 16 * 8)
            g.wait_ge(sKW1, 16 * 8)
            for arm in nc.switch_core_id(engines=[g], n=128):
                fqq = (arm.logical % 4) ^ 2
                g.dma_start(out=_ap(kfmS, [[128 * 4 * T, 8], [4 * T, 128], [1, T]], fqq * T),
                            in_=kfmP[1]).then_inc(sRL2, 16)
            g.wait_ge(vch, vch_fv_e(15))
            g.wait_ge(sRGg, 16 * 24)
            for h in range(1, 16, 2):
                vregroup(g, h, 3, sRGg)
            g.wait_ge(sRGg, 16 * 32)
            g.wait_ge(sRGs, 16 * 24)
            g.wait_ge(sRG3, 16 * 8)
            for arm in nc.switch_core_id(engines=[g], n=128):
                fqq = (arm.logical % 4) ^ 2
                g.dma_start(out=_ap(kvsh, [[WIN * DF, 8], [1, WQ * DF]],
                                    8 * WIN * DF + fqq * WQ * DF),
                            in_=kvf[8:16]).then_inc(sRL3, 16)
            gate_l2(g)
            g.wait_ge(sRL, 32)
            g.wait_ge(sRL2, 16)
            g.wait_ge(sRL3, 16)
            g.wait_ge(sRL4, 16)
            g.dma_start(out=cbin[:], in_=ident_dram[0:1, 0:64], allow_slow_non_contiguous=True).then_inc(sCB, 16)
            g.wait_ge(sCB, 16)
            g.collective_compute("AllGather", mybir.AluOpType.bypass,
                                 replica_groups=[[0, 1], [2, 3], [4, 5], [6, 7]],
                                 ins=[cbin[:]], outs=[cbout[0:2]]).then_inc(sBAR, 1)
            for cc in range(8):
                g.dma_start(out=wpb[:, cc * D:(cc + 1) * D],
                            in_=wprojT[cc * 128:(cc + 1) * 128, :]).then_inc(sW2, 16)
            for oc in range(8):
                g.wait_ge(vch2, v2_rs(oc))
                src = _ap(PD[:], [[7216, 128], [1, 1]], 7204 + oc)
                dst = _ap(PR[:], [[16384, 1], [1, 128]], 192 + oc * 128)
                g.dma_start(out=dst, in_=src).then_inc(sRR, 16)
            g.wait_ge(sBAR, 1)

            def vload(h):
                if h >= 2:
                    g.wait_ge(tch2, t2_PV(h - 2, 3))
                par = h % 2
                for wc in range(2):
                    g.dma_start(out=vb[:, par * 2 * DF + wc * DF:par * 2 * DF + (wc + 1) * DF],
                                in_=kvsh[h, wc * 128:(wc + 1) * 128, :]).then_inc(
                        sV0 if par == 0 else sV1, 16)

            vload(0)
            vload(1)
            for h in range(H):
                g.wait_ge(vch2, v2_hend(h))
                if h >= 1:
                    g.wait_ge(sOG, 16 * h)
                src = _ap(PA[:], [[PAW, 64], [64, 32], [1, 64]], 16384 + (h % 2) * DF)
                dst = _ap(otm, [[32 * D, 64], [D, 32], [1, 64]], h * 64)
                g.dma_start(out=dst, in_=src).then_inc(sOG, 16)
                if h + 2 < H:
                    vload(h + 2)
            for tc in range(NTC):
                par = tc % 2
                g.wait_ge(sch2, s4_cast(tc))
                if tc >= 2:
                    g.wait_ge(sQL0 if par == 0 else sQL1, 16 * (tc // 2))
                g.tensor_copy(qbb[:, par * D:(par + 1) * D], zib[:, :]).then_inc(gch, 1)
                g.wait_ge(gch, tc + 1)
            g.wait_ge(sY0, 16 * 16)

        @block.scalar
        def _(s):
            s.dma_start(out=identT[:], in_=ident_dram[:]).then_inc(sID, 16)
            s.wait_ge(vch, 1)
            for i in range(NT):
                p = (i % 2) * 8
                par = (i % 2) * D
                s.wait_ge(sX0 if i % 2 == 0 else sX1, 16 * (i // 2 + 1))
                if i >= 2:
                    s.wait_ge(vch, vch_nm(i - 2))
                s.activation(sqj[:, :], xt[:, par:par + D], AF.Copy,
                             accum_out=st[:, p + 0:p + 1]).then_inc(sch, 1)
                s.wait_ge(vch, vch_nm(i))
                s.activation(sqj[:, :], xt[:, par:par + D], AF.Square,
                             bias=st[:, p + 2:p + 3],
                             accum_out=st[:, p + 1:p + 2]).then_inc(sch, 1)
                s.wait_ge(sch, sch_sq2(i))
                s.activation(st[:, p + 6:p + 7], st[:, p + 1:p + 2], AF.Sqrt,
                             scale=1.0 / D, bias=ceps[:, 0:1]).then_inc(sch, 1)
            s.wait_ge(sRGg, 16 * 16)
            s.wait_ge(sRGs, 16 * 16)
            s.wait_ge(sKW0, 16 * 4)
            s.wait_ge(sKW1, 16 * 4)
            for arm in nc.switch_core_id(engines=[s], n=128):
                qq = arm.logical % 4
                s.dma_start(out=_ap(kvsh, [[WIN * DF, H], [1, WQ * DF]], qq * WQ * DF),
                            in_=kvo[:]).then_inc(sRL, 16)
                s.dma_start(out=_ap(kfmS, [[128 * 4 * T, 8], [4 * T, 128], [1, T]], qq * T),
                            in_=kfmP[0]).then_inc(sRL, 16)
            s.wait_ge(vch, vch_fv_e(15))
            for h in range(0, 16, 2):
                vregroup(s, h, 3, sRG3)
            gate_l2(s)
            scnt = [0]

            def sop(inst):
                inst.then_inc(sch2, 1)
                scnt[0] += 1
                s.wait_ge(sch2, scnt[0])

            for oc in range(8):
                s.wait_ge(sRQ0, 16 * (oc + 1))
                if oc >= 1:
                    s.wait_ge(vch2, v2_qb(oc - 1))
                sop(s.activation(sgnb[:, :], rqt[:, :], AF.Sign))
                assert scnt[0] == s2_sgn(oc)
            s.wait_ge(sBAR, 1)
            s.dma_start(out=kfr[:, 4096:8192],
                        in_=kfmS[0][:, 4096:8192]).then_inc(sKH0, 16)
            for h in range(H):
                s.wait_ge(tch2, t2_S(h))
                s.wait_ge(vch2, v2_nb(h))
                if h >= 1:
                    s.wait_ge(tch2, t2_PT(h - 1, 1))
                    s.wait_ge(vch2, v2_recip(h - 1))
                sop(s.activation(pb[:, :], psS[:, :], AF.Exp, scale=0.125, bias=nb1[:, :],
                                 accum_out=rsum[:, :]))
                assert scnt[0] == s2_exp(h)
                if h % 2 == 1 and h <= 13:
                    p = (h + 1) // 2
                    s.dma_start(out=kfr[:, 4096:8192],
                                in_=kfmS[p][:, 4096:8192]).then_inc(sKH0 if p % 2 == 0 else sKH1, 16)
            for tc in range(NTC):
                s.wait_ge(tch2, t3(tc))
                if tc >= 1:
                    s.wait_ge(vch2, v3_norm(tc - 1))
                sop(s.activation(zrt[:, :], psP[:, :], AF.Copy, accum_out=st2[:, 0:1]))
                s.wait_ge(vch2, v3_nm(tc))
                sop(s.activation(ln2t[:, :], zrt[:, :], AF.Square, bias=st2[:, 2:3],
                                 accum_out=st2[:, 1:2]))
                s.wait_ge(sch2, s3_sq(tc))
                sop(s.activation(st2[:, 6:7], st2[:, 1:2], AF.Sqrt, scale=1.0 / D,
                                 bias=ceps2[:, 0:1]))
                assert scnt[0] == s3_sd(tc)
            for tc in range(NTC):
                s.wait_ge(vch2, v4_nm(tc))
                pns = 10 + (tc % 2)
                sop(s.activation(zrt[:, :], x1[:, tc * D:(tc + 1) * D], AF.Square,
                                 bias=st2[:, pns:pns + 1], accum_out=st2[:, 9:10]))
                s.wait_ge(sch2, s4_sq(tc))
                sop(s.activation(st2[:, 14:15], st2[:, 9:10], AF.Sqrt, scale=1.0 / D,
                                 bias=ceps2[:, 0:1]))
                s.wait_ge(vch2, v4_ln2(tc))
                if tc >= 1:
                    s.wait_ge(gch, tc)
                else:
                    s.wait_ge(vch2, v2_qb(7))
                sop(s.activation(sgnb[:, :], ln2t[:, :], AF.Sign))
                assert scnt[0] == s4_sgn(tc)
                sop(s.mul(sgnb[:, :], sgnb[:, :], 0.5))
                assert scnt[0] == s4_half(tc)
                s.wait_ge(vch2, v4_cast(tc))
                if tc >= 1:
                    s.wait_ge(gch, tc)
                sop(s.activation(zib[:, :], ln2t[:, :], AF.Copy))
                assert scnt[0] == s4_cast(tc)
            s.wait_ge(sQL0, 16 * 8)
            s.wait_ge(sQL1, 16 * 8)
            for tc in range(NTC):
                s.wait_ge(vch2, v5_nm(tc))
                sop(s.activation(qbb[:, 0:D], zrt[:, :], AF.Square, bias=st2[:, 10:11],
                                 accum_out=st2[:, 9:10]))
                s.wait_ge(vch2, v5_vp(tc))
                sop(s.activation(st2[:, 14:15], st2[:, 13:14], AF.Sqrt, bias=ceps2[:, 0:1]))
                assert scnt[0] == s5_sd(tc)
            s.wait_ge(sY0, 16 * 16)

        @block.tensor
        def _(t):
            t.wait_ge(sID, 16)
            for _j in range(250):
                mm = t.matmul(psBT[:, 0:128], ident[0:1, 0:128], ident[0:1, 0:128],
                              start=True, stop=True)
            mm.then_inc(sJ1, 1)
            t.wait_ge(sW, 16 * 8)
            tcnt1 = [0]

            def top1(inst):
                inst.then_inc(tch, 1)
                tcnt1[0] += 1

            t.wait_ge(sFMo, 16 * 8)
            for k in range(NTC):
                if k >= 1:
                    t.wait_ge(vch, vch_fmevac(k - 1))
                for cc in range(8):
                    statn = wq_b[:, cc * G3 + k * 128: cc * G3 + (k + 1) * 128]
                    for ts in range(4):
                        mm = t.matmul(psA[:, ts * 512:(ts + 1) * 512], statn,
                                      fmo[:, cc * T + ts * 512: cc * T + (ts + 1) * 512],
                                      start=(cc == 0), stop=(cc == 7))
                        if cc == 7:
                            top1(mm)
                assert tcnt1[0] == tch_fm(k)
                if k >= 1:
                    t.wait_ge(vch, vch_evac2(k - 1))
                else:
                    t.wait_ge(sJ1, 1)
                for cc in range(8):
                    statn = fmo[:, cc * T + k * 128: cc * T + (k + 1) * 128]
                    for vc in range(2):
                        mm = t.matmul(psB[:, vc * 512:(vc + 1) * 512], statn,
                                      wq_b[:, cc * G3 + 2048 + vc * 512: cc * G3 + 2048 + (vc + 1) * 512],
                                      start=(cc == 0), stop=(cc == 7))
                        if cc == 7:
                            top1(mm)
                assert tcnt1[0] == tch_v(k)
            t.wait_ge(sFMf, 16 * 8)
            for k in range(8):
                t.wait_ge(vch, vch_fmevac(15) if k == 0 else vch_ffm_e(k - 1))
                for cc in range(8):
                    statn = wq_b[:, cc * G3 + 1024 + k * 128: cc * G3 + 1024 + (k + 1) * 128]
                    for ts in range(4):
                        mm = t.matmul(psA[:, ts * 512:(ts + 1) * 512], statn,
                                      fmf[:, cc * T + ts * 512: cc * T + (ts + 1) * 512],
                                      start=(cc == 0), stop=(cc == 7))
                        if cc == 7:
                            top1(mm)
                assert tcnt1[0] == tch_ffm(k)
            for k in range(NTC):
                t.wait_ge(vch, vch_evac2(15) if k == 0 else vch_fv_e(k - 1))
                for cc in range(8):
                    statn = fmf[:, cc * T + k * 128: cc * T + (k + 1) * 128]
                    for vc in range(2):
                        mm = t.matmul(psB[:, vc * 512:(vc + 1) * 512], statn,
                                      wq_b[:, cc * G3 + 2048 + vc * 512: cc * G3 + 2048 + (vc + 1) * 512],
                                      start=(cc == 0), stop=(cc == 7))
                        if cc == 7:
                            top1(mm)
                assert tcnt1[0] == tch_fv(k)
            t.wait_ge(vch, 257)
            for _j in range(400):
                mm = t.matmul(psBT[:, 0:128], ident[0:1, 0:128], ident[0:1, 0:128],
                              start=True, stop=True)
            mm.then_inc(sJ2, 1)
            gate_l2(t)
            tcnt = [0]

            def top(inst):
                inst.then_inc(tch2, 1)
                tcnt[0] += 1

            t.wait_ge(vch2, 2)
            t.wait_ge(sRR, 16 * 8)
            top(t.matmul(psP[:, 0:512], onesf_ap, rsrow[:, 0:512], start=True, stop=True))
            top(t.matmul(psP[:, 512:1024], onesf_ap, rsrow[:, 512:1024], start=True, stop=True))
            for _j in range(120):
                t.matmul(psAT[:, 1792:1920], ident[0:1, 0:128], ident[0:1, 0:128],
                         start=True, stop=True)
            for h in range(H):
                p = h // 2
                rB = (h % 2) * 64
                t.wait_ge(sKF0 if p % 2 == 0 else sKF1, 16 * (p // 2 + 1))
                t.wait_ge(sKH0 if p % 2 == 0 else sKH1, 16 * (p // 2 + 1))
                if h >= 1:
                    t.wait_ge(sch2, s2_exp(h - 1))
                for j in range(32):
                    qsl = _ap(qfm, [[16384, 64], [32, 64]], rB * 16384 + p * 2048 + j)
                    ksl = _ap(kfr, [[PAW, 64], [32, 256]], rB * PAW + j)
                    mm = t.matmul(psS[:, :], qsl, ksl, start=(j == 0), stop=(j == 31))
                    if j == 31:
                        top(mm)
                assert tcnt[0] == t2_S(h)
                t.wait_ge(sch2, s2_exp(h))
                t.wait_ge(sID, 16)
                if h >= 1:
                    t.wait_ge(vch2, v2_pt(h - 1, 1))
                top(t.transpose(psPT[:, :], pb[:, 0:128], ident[0:64, 0:64]))
                t.wait_ge(vch2, v2_pt(h, 0))
                top(t.transpose(psPT[:, :], pb[:, 128:256], ident[0:64, 0:64]))
                par = h % 2
                if h == 0:
                    t.wait_ge(sJ2, 1)
                t.wait_ge(sV0 if par == 0 else sV1, 16 * 2 * (h // 2 + 1))
                t.wait_ge(vch2, v2_pt(h, 1))
                if h >= 1:
                    t.wait_ge(vch2, v2_hend(h - 1))
                for fc in range(4):
                    for wc in range(2):
                        mm = t.matmul(psO[:, fc * 512:(fc + 1) * 512],
                                      ptb[:, wc * 64:(wc + 1) * 64],
                                      vb[:, par * 2 * DF + wc * DF + fc * 512:par * 2 * DF + wc * DF + fc * 512 + 512],
                                      start=(wc == 0), stop=(wc == 1))
                        if wc == 1:
                            top(mm)
                assert tcnt[0] == t2_hend(h)
            for _j in range(2500):
                t.matmul(psAT[:, 1792:1920], ident[0:1, 0:128], ident[0:1, 0:128],
                         start=True, stop=True)
            t.wait_ge(sOF, 16 * 8)
            t.wait_ge(sW2, 16 * 8)
            t.wait_ge(vch2, V2_RSB)
            for tc in range(NTC):
                if tc >= 1:
                    t.wait_ge(sch2, s3_sum(tc - 1))
                for nc2 in range(2):
                    for cc in range(8):
                        mm = t.matmul(psP[:, nc2 * 512:(nc2 + 1) * 512],
                                      ofm[:, cc * T + tc * 128:cc * T + (tc + 1) * 128],
                                      wpb[:, cc * D + nc2 * 512:cc * D + nc2 * 512 + 512],
                                      start=(cc == 0), stop=(cc == 7))
                        if cc == 7:
                            top(mm)
                assert tcnt[0] == t3(tc)
            t.wait_ge(sQF, 16 * 8)
            t.wait_ge(sQR, 16 * 8)
            t.wait_ge(sch2, s3_sum(NTC - 1))
            for tc in range(NTC):
                if tc >= 1:
                    t.wait_ge(vch2, v5_zr(tc - 1))
                for nc2 in range(2):
                    for cc in range(8):
                        mm = t.matmul(psP[:, nc2 * 512:(nc2 + 1) * 512],
                                      ofm[:, cc * T + tc * 128:cc * T + (tc + 1) * 128],
                                      qrs[:, cc * D + nc2 * 512:cc * D + nc2 * 512 + 512],
                                      start=(cc == 0), stop=(cc == 7))
                        if cc == 7:
                            top(mm)
                assert tcnt[0] == t5(tc)

        @block.vector
        def _(v):
            vcnt = [0]

            def vop(inst):
                inst.then_inc(vch, 1)
                vcnt[0] += 1
                v.wait_ge(vch, vcnt[0])

            vop(v.memset(ceps[:], EPS))

            def ln_v(i):
                p = (i % 2) * 8
                v.wait_ge(sch, sch_sum(i))
                vop(v.tensor_scalar(st[:, p + 2:p + 3], st[:, p + 0:p + 1], -1.0 / D, None, OP.mult))
                assert vcnt[0] == vch_nm(i), (i, vcnt[0])
                v.wait_ge(sch, sch_sd2(i))
                vop(v.reciprocal(st[:, p + 6:p + 7], st[:, p + 6:p + 7]))
                if i >= 2:
                    v.wait_ge(sTW0 if i % 2 == 0 else sTW1, 16 * (i // 2))
                vop(v.tensor_scalar(lnt[:, (i % 2) * D:(i % 2 + 1) * D],
                                    xt[:, (i % 2) * D:(i % 2 + 1) * D],
                                    st[:, p + 2:p + 3], st[:, p + 6:p + 7], OP.add, OP.mult))
                assert vcnt[0] == vch_norm2(i), (i, vcnt[0])

            def fm_evac(k, own):
                v.wait_ge(tch, tch_fm(k) if own else tch_ffm(k))
                if own and k < 8:
                    dst = lambda ts: qfm[:, k * 2048 + ts * 512: k * 2048 + (ts + 1) * 512]
                else:
                    kp = (k - 8) if own else k
                    nprev = (kp // 2) if own else (4 + kp // 2)
                    if nprev > 0:
                        v.wait_ge(sKW0 if kp % 2 == 0 else sKW1, 16 * nprev)
                    dst = lambda ts: kfmb[:, 8192 + (kp % 2) * 2048 + ts * 512: 8192 + (kp % 2) * 2048 + (ts + 1) * 512]
                for ts in range(4):
                    vop(v.tensor_copy(dst(ts), psA[:, ts * 512:(ts + 1) * 512]))

            def v_evac(k, own):
                if own and k >= 8:
                    v.wait_ge(sRGg, 16 * 8)
                    v.wait_ge(sRGs, 16 * 8)
                if (not own) and k >= 8:
                    v.wait_ge(sRGg, 16 * 24)
                    v.wait_ge(sRGs, 16 * 24)
                v.wait_ge(tch, tch_v(k) if own else tch_fv(k))
                slot = k % 8
                base = 0 if own else 12288
                for vc in range(2):
                    vop(v.tensor_copy(ringv[:, base + slot * 1024 + vc * 512: base + slot * 1024 + (vc + 1) * 512],
                                      psB[:, vc * 512:(vc + 1) * 512]))

            for i in range(NTC):
                ln_v(i)
            for k in range(NTC):
                ln_v(NTC + k)
                fm_evac(k, True)
                v_evac(k, True)
                assert vcnt[0] == vch_evac2(k), (k, vcnt[0])
            for k in range(8):
                fm_evac(k, False)
                assert vcnt[0] == vch_ffm_e(k), (k, vcnt[0])
            for k in range(NTC):
                v_evac(k, False)
                assert vcnt[0] == vch_fv_e(k), (k, vcnt[0])
            gate_l2(v)
            vcnt2 = [0]

            def vop2(inst):
                inst.then_inc(vch2, 1)
                vcnt2[0] += 1
                v.wait_ge(vch2, vcnt2[0])

            vop2(v.memset(onesf_ap, 1.0))
            vop2(v.memset(ceps2, EPS))
            for oc in range(8):
                v.wait_ge(sRQ0, 16 * (oc + 1))
                vop2(v.reduce_max(st2[:, 0:1], rqt[:, :], axis=AX.X,
                                  apply_absolute_value=True))
                vop2(v.tensor_scalar(rsT[:, oc:oc + 1], st2[:, 0:1], 1.0 / 127.0, None, OP.mult))
                assert vcnt2[0] == v2_rs(oc)
                vop2(v.reciprocal(st2[:, 1:2], rsT[:, oc:oc + 1]))
                vop2(v.tensor_scalar(ln2t[:, :], rqt[:, :],
                                     st2[:, 1:2], None, OP.mult))
                v.wait_ge(sch2, s2_sgn(oc))
                vop2(v.scalar_tensor_tensor(ln2t[:, :], sgnb[:, :], 0.5, ln2t[:, :], OP.mult, OP.add))
                vop2(v.tensor_scalar(ln2t[:, :], ln2t[:, :], 127.49, -127.49, OP.min, OP.max))
                vop2(v.tensor_copy(zib[:, :], ln2t[:, :]))
                if oc >= 2:
                    v.wait_ge(sQW0 if oc % 2 == 0 else sQW1, 16 * (oc // 2))
                vop2(v.tensor_copy(qbb[:, (oc % 2) * D:(oc % 2 + 1) * D], zib[:, :]))
                assert vcnt2[0] == v2_qb(oc)
            v.wait_ge(tch2, 2)
            vop2(v.tensor_copy(rsb[:, :], psP[:, :]))
            assert vcnt2[0] == V2_RSB
            for h in range(H):
                v.wait_ge(tch2, t2_S(h))
                if h >= 1:
                    v.wait_ge(sch2, s2_exp(h - 1))
                vop2(v.reduce_max(smax[:, :], psS[:, :], axis=AX.X))
                assert vcnt2[0] == v2_smax(h)
                vop2(v.tensor_scalar(nb1[:, :], smax[:, :], -0.125, None, OP.mult))
                v.wait_ge(sch2, s2_exp(h))
                vop2(v.reciprocal(rr[:, :], rsum[:, :]))
                assert vcnt2[0] == v2_recip(h)
                for k in range(2):
                    v.wait_ge(tch2, t2_PT(h, k))
                    vop2(v.tensor_copy(ptb[:, k * 64:(k + 1) * 64], psPT[:, :]))
                v.wait_ge(tch2, t2_PV(h, 3))
                if h >= 2:
                    v.wait_ge(sOG, 16 * (h - 1))
                for fc in range(4):
                    vop2(v.tensor_scalar(owb[:, (h % 2) * DF + fc * 512:(h % 2) * DF + fc * 512 + 512],
                                         psO[:, fc * 512:(fc + 1) * 512],
                                         rr[:, 0:1], None, OP.mult))
                assert vcnt2[0] == v2_hend(h)
            for tc in range(NTC):
                if tc == 0:
                    v.wait_ge(sOG, 16 * 16)
                v.wait_ge(sch2, s3_sum(tc))
                vop2(v.tensor_scalar(st2[:, 2:3], st2[:, 0:1], -1.0 / D, None, OP.mult))
                assert vcnt2[0] == v3_nm(tc)
                v.wait_ge(sch2, s3_sd(tc))
                vop2(v.reciprocal(st2[:, 6:7], st2[:, 6:7]))
                vop2(v.tensor_scalar(nrm[:, :], zrt[:, :], st2[:, 2:3], st2[:, 6:7],
                                     OP.add, OP.mult))
                assert vcnt2[0] == v3_norm(tc)
                par = tc % 2
                v.wait_ge(sX20 if par == 0 else sX21, 16 * (tc // 2 + 1))
                vop2(v.scalar_tensor_tensor(x1[:, tc * D:(tc + 1) * D], nrm[:, :], 0.0,
                                            xt2[:, par * D:(par + 1) * D], OP.add, OP.add,
                                            accum_out=xsum[:, tc:tc + 1]))
                assert vcnt2[0] == v3_end(tc)
            vop2(v.tensor_scalar(st2[:, 10:11], xsum[:, 0:1], -1.0 / D, None, OP.mult))
            assert vcnt2[0] == v4_nm(0)
            for tc in range(NTC):
                pn = 10 + (tc % 2)
                v.wait_ge(sch2, s4_sd(tc))
                vop2(v.reciprocal(st2[:, 14:15], st2[:, 14:15]))
                vop2(v.tensor_scalar(ln2t[:, :], x1[:, tc * D:(tc + 1) * D],
                                     st2[:, pn:pn + 1], st2[:, 14:15], OP.add, OP.mult))
                assert vcnt2[0] == v4_ln2(tc)
                if tc + 1 < NTC:
                    pn1 = 10 + ((tc + 1) % 2)
                    if tc >= 1:
                        v.wait_ge(sch2, s4_sq(tc - 1))
                    vop2(v.tensor_scalar(st2[:, pn1:pn1 + 1], xsum[:, tc + 1:tc + 2],
                                         -1.0 / D, None, OP.mult))
                    assert vcnt2[0] == v4_nm(tc + 1)
                vop2(v.reduce_max(st2[:, 0:1], ln2t[:, :], axis=AX.X, apply_absolute_value=True))
                vop2(v.reciprocal(st2[:, 1:2], st2[:, 0:1]))
                vop2(v.tensor_scalar(st2[:, 2:3], st2[:, 1:2], 127.0, None, OP.mult))
                vop2(v.tensor_scalar(lst[:, tc:tc + 1], st2[:, 0:1], 1.0 / 127.0, None, OP.mult))
                v.wait_ge(sch2, s4_half(tc))
                vop2(v.scalar_tensor_tensor(ln2t[:, :], ln2t[:, :], st2[:, 2:3], sgnb[:, :],
                                            OP.mult, OP.add))
                vop2(v.tensor_scalar(ln2t[:, :], ln2t[:, :], 127.49, -127.49, OP.min, OP.max))
                assert vcnt2[0] == v4_cast(tc)
            for tc in range(NTC):
                if tc == 0:
                    v.wait_ge(sch2, S2_P4END)
                else:
                    v.wait_ge(sch2, s5_sq(tc - 1))
                v.wait_ge(tch2, t5(tc))
                vop2(v.scalar_tensor_tensor(zrt[:, :], psP[:, :], 0.0, rsb[:, :],
                                            OP.add, OP.mult, accum_out=st2[:, 8:9]))
                assert vcnt2[0] == v5_zr(tc)
                vop2(v.tensor_scalar(st2[:, 10:11], st2[:, 8:9], -1.0 / D, None, OP.mult))
                assert vcnt2[0] == v5_nm(tc)
                vop2(v.tensor_tensor(st2[:, 4:5], lst[:, tc:tc + 1], lst[:, tc:tc + 1], OP.mult))
                v.wait_ge(sch2, s5_sq(tc))
                vop2(v.tensor_scalar(st2[:, 13:14], st2[:, 9:10], 1.0 / D, None, OP.mult))
                vop2(v.tensor_tensor(st2[:, 13:14], st2[:, 13:14], st2[:, 4:5], OP.mult))
                assert vcnt2[0] == v5_vp(tc)
                v.wait_ge(sch2, s5_sd(tc))
                vop2(v.reciprocal(st2[:, 14:15], st2[:, 14:15]))
                vop2(v.tensor_tensor(st2[:, 5:6], lst[:, tc:tc + 1], st2[:, 14:15], OP.mult))
                if tc >= 1:
                    v.wait_ge(sY0, 16 * tc)
                vop2(v.tensor_scalar(nrm[:, :], zrt[:, :], st2[:, 10:11], st2[:, 5:6],
                                     OP.add, OP.mult))
                vop2(v.tensor_tensor(nrm[:, :], nrm[:, :],
                                     x1[:, tc * D:(tc + 1) * D], OP.add))
                assert vcnt2[0] == v5_y(tc)
    return nc


def kernel(L0_attn_x, L0_attn_Wqkv, L0_attn_Wproj, L0_ffn_rhs):
    x = np.ascontiguousarray(np.asarray(L0_attn_x, dtype=np.float32))
    wqkvT = np.ascontiguousarray(np.asarray(L0_attn_Wqkv, dtype=np.float32).T)
    wprojT = np.ascontiguousarray(np.asarray(L0_attn_Wproj, dtype=np.float32).T)
    rhsT = np.ascontiguousarray(np.asarray(L0_ffn_rhs, dtype=np.float32).T)

    nc = build_fused()
    in_maps = []
    for c in range(8):
        b, q = c // 4, c % 4
        fq = q ^ 2
        in_maps.append({
            "xq": np.ascontiguousarray(x[b, q * T:(q + 1) * T]),
            "xf": np.ascontiguousarray(x[b, fq * T:(fq + 1) * T]),
            "wqkvT": wqkvT,
            "wprojT": wprojT,
            "rhsT": rhsT,
        })
    res = run_bass_kernel_spmd(nc, in_maps, core_ids=list(range(8)))
    out = np.empty((2, 8192, D), dtype=np.float32)
    for c in range(8):
        b, q = c // 4, c % 4
        out[b, q * T:(q + 1) * T] = np.asarray(res.results[c]["y"])
    return out


# version 26
# speedup vs baseline: 1.1776x; 1.1776x over previous
import sys
import numpy as np
import ml_dtypes
import contextlib

sys.path.insert(0, '/opt/trn_rl_repo')
import concourse.bass as bass
import concourse.mybir as mybir
from concourse.alu_op_type import AluOpType as OP
from concourse.bass_utils import run_bass_kernel_spmd

F32 = mybir.dt.float32
BF16 = mybir.dt.bfloat16
I32 = mybir.dt.int32
AF = mybir.ActivationFunctionType
AX = mybir.AxisListType
BFNP = ml_dtypes.bfloat16

D = 1024
H = 16
HD = 64
T = 2048
NTC = T // 128
G3 = 3 * D
WQ = 64
WIN = 256
DF = 2048
SLOTS = 8
RS = SLOTS * G3
EPS = 1e-5
NT = 2 * NTC


def _ap(base, dims, offset):
    a = base.copy()
    a.ap = type(a.ap)([list(d) for d in dims])
    a.offset = offset
    return a


def vch_norm(i):
    return 6 * i + 7 if i < NTC else 103 + 12 * (i - NTC)


def vch_cen(i):
    return 6 * i + 4 if i < NTC else 100 + 12 * (i - NTC)


def vch_var(i):
    return 6 * i + 5 if i < NTC else 101 + 12 * (i - NTC)


def vch_evac(i):
    return 109 + 12 * i if i < NTC else 293 + 4 * (i - NTC)


def sch_sq(i):
    return 2 * i + 1


def sch_sd(i):
    return 2 * i + 2


def _tc_lens(i):
    return (3, 3) if i < NTC else (2, 2)


def tch_A(i):
    c = 0
    for j in range(i):
        a, b = _tc_lens(j)
        c += a + b
    return c + _tc_lens(i)[0]


def tch_B(i):
    return tch_A(i) + _tc_lens(i)[1]


def build_l1():
    nc = bass.Bass("TRN2", target_bir_lowering=False)
    xq = nc.dram_tensor("xq", [T, D], F32, kind="ExternalInput").ap()
    xf = nc.dram_tensor("xf", [T, D], F32, kind="ExternalInput").ap()
    wqkvT = nc.dram_tensor("wqkvT", [D, G3], F32, kind="ExternalInput").ap()
    qwin = nc.dram_tensor("qwin", [H, WQ, DF], BF16, kind="ExternalOutput").ap()
    kvo = nc.dram_tensor("kvo", [2, H, WQ, DF], BF16, kind="ExternalOutput").ap()
    kvf = nc.dram_tensor("kvf", [2, H, WQ, DF], BF16, kind="ExternalOutput").ap()
    ln_tm = nc.dram_tensor("ln_tm", [2, T, D], BF16).ap()

    with contextlib.ExitStack() as ES:
        E = ES.enter_context
        wq_b = E(nc.sbuf_tensor("wq_b", [128, 8 * G3], BF16))
        xt = E(nc.sbuf_tensor("xt", [128, 2 * D], F32))
        lnt = E(nc.sbuf_tensor("lnt", [128, 2 * D], BF16))
        cen = E(nc.sbuf_tensor("cen", [128, 2 * D], F32))
        sqj = E(nc.sbuf_tensor("sqj", [128, D], F32))
        st = E(nc.sbuf_tensor("st", [128, 16], F32))
        ceps = E(nc.sbuf_tensor("ceps", [128, 1], F32))
        fmo = E(nc.sbuf_tensor("fmo", [128, 8 * T], BF16))
        fmf = E(nc.sbuf_tensor("fmf", [128, 8 * T], BF16))
        ring = E(nc.sbuf_tensor("ring", [128, RS], BF16))
        psA = E(nc.psum_tensor("psA", [128, 3 * 512], F32))
        psB = E(nc.psum_tensor("psB", [128, 3 * 512], F32))

        sX0 = E(nc.semaphore("sX0"))
        sX1 = E(nc.semaphore("sX1"))
        sW = E(nc.semaphore("sW"))
        sTW0 = E(nc.semaphore("sTW0"))
        sTW1 = E(nc.semaphore("sTW1"))
        sFMo = E(nc.semaphore("sFMo"))
        sFMf = E(nc.semaphore("sFMf"))
        sRGg = E(nc.semaphore("sRGg"))
        sRGs = E(nc.semaphore("sRGs"))
        vch = E(nc.semaphore("vch"))
        sch = E(nc.semaphore("sch"))
        tch = E(nc.semaphore("tch"))
        block = E(nc.Block())

        goffA = lambda i: ([0, 512, 1024] if i < NTC else [1024, 1536])
        goffB = lambda i: ([1536, 2048, 2560] if i < NTC else [2048, 2560])
        sh_even = [len([sh for sh in (range(48) if g < 2 else range(16, 48)) if sh % 2 == 0]) for g in range(4)]
        sh_odd = [len([sh for sh in (range(48) if g < 2 else range(16, 48)) if sh % 2 == 1]) for g in range(4)]
        gcum = [0]
        scum = [0]
        for g in range(4):
            gcum.append(gcum[-1] + sh_even[g])
            scum.append(scum[-1] + sh_odd[g])

        @block.sync
        def _(sy):
            def load(i):
                src = xq if i < NTC else xf
                tl = i % NTC
                if i >= 2:
                    sy.wait_ge(vch, vch_norm(i - 2))
                    sy.wait_ge(sch, sch_sq(i - 2))
                sy.dma_start(out=xt[:, (i % 2) * D:(i % 2 + 1) * D],
                             in_=src[tl * 128:(tl + 1) * 128, :]).then_inc(sX0 if i % 2 == 0 else sX1, 16)

            def write(j):
                sy.wait_ge(vch, vch_norm(j))
                sy.dma_start(out=ln_tm[j // NTC, (j % NTC) * 128:(j % NTC + 1) * 128, :],
                             in_=lnt[:, (j % 2) * D:(j % 2 + 1) * D]).then_inc(sTW0 if j % 2 == 0 else sTW1, 16)

            for i in range(NTC):
                load(i)
                if i >= 1:
                    write(i - 1)
            write(NTC - 1)
            sy.wait_ge(sTW0, 16 * (NTC // 2))
            sy.wait_ge(sTW1, 16 * (NTC // 2))
            for k in range(8):
                sy.dma_start(out=fmo[:, k * T:(k + 1) * T],
                             in_=ln_tm[0, :, k * 128:(k + 1) * 128], transpose=True).then_inc(sFMo, 16)
            def rg_grp(grp):
                own = grp < 2
                sy.wait_ge(vch, vch_evac(grp * 8 + 7))
                if grp >= 1:
                    sy.wait_ge(sRGs, 16 * scum[grp])
                for sh in [sh for sh in (range(48) if own else range(16, 48)) if sh % 2 == 1]:
                    _emit_regroup(sy, sh, grp, ring, qwin, kvo, kvf, sRGs)

            for i in range(NTC, NT):
                load(i)
                if i >= NTC + 1:
                    write(i - 1)
                if i == NTC + 9:
                    rg_grp(0)
            write(NT - 1)
            rg_grp(1)
            sy.wait_ge(sTW0, 16 * (NT // 2))
            sy.wait_ge(sTW1, 16 * (NT // 2))
            for k in range(8):
                sy.dma_start(out=fmf[:, k * T:(k + 1) * T],
                             in_=ln_tm[1, :, k * 128:(k + 1) * 128], transpose=True).then_inc(sFMf, 16)
            rg_grp(2)
            rg_grp(3)
            sy.wait_ge(sRGs, 16 * scum[4])
            sy.wait_ge(sRGg, 16 * gcum[4])

        @block.gpsimd
        def _(g):
            for cc in range(8):
                g.dma_start(out=wq_b[:, cc * G3:(cc + 1) * G3],
                            in_=wqkvT[cc * 128:(cc + 1) * 128, :]).then_inc(sW, 16)
            for grp in range(4):
                own = grp < 2
                g.wait_ge(vch, vch_evac(grp * 8 + 7))
                if grp >= 1:
                    g.wait_ge(sRGg, 16 * gcum[grp])
                for sh in [sh for sh in (range(48) if own else range(16, 48)) if sh % 2 == 0]:
                    _emit_regroup(g, sh, grp, ring, qwin, kvo, kvf, sRGg)
            g.wait_ge(sRGg, 16 * gcum[4])
            g.wait_ge(sRGs, 16 * scum[4])

        @block.scalar
        def _(s):
            for i in range(NT):
                p = (i % 2) * 8
                s.wait_ge(vch, vch_cen(i))
                s.activation(sqj[:, :], cen[:, (i % 2) * D:(i % 2 + 1) * D], AF.Square,
                             accum_out=st[:, p + 1:p + 2]).then_inc(sch, 1)
                s.wait_ge(vch, vch_var(i))
                s.activation(st[:, p + 6:p + 7], st[:, p + 5:p + 6], AF.Sqrt,
                             bias=ceps[:, 0:1]).then_inc(sch, 1)

        @block.vector
        def _(v):
            vcnt = [0]

            def vop(inst):
                inst.then_inc(vch, 1)
                vcnt[0] += 1
                v.wait_ge(vch, vcnt[0])

            vop(v.memset(ceps[:], EPS))

            def ln_v(i):
                p = (i % 2) * 8
                v.wait_ge(sX0 if i % 2 == 0 else sX1, 16 * (i // 2 + 1))
                vop(v.reduce_sum(st[:, p + 0:p + 1], xt[:, (i % 2) * D:(i % 2 + 1) * D], axis=AX.X))
                vop(v.tensor_scalar(st[:, p + 2:p + 3], st[:, p + 0:p + 1], 1.0 / D, None, OP.mult))
                vop(v.tensor_scalar(cen[:, (i % 2) * D:(i % 2 + 1) * D],
                                    xt[:, (i % 2) * D:(i % 2 + 1) * D],
                                    st[:, p + 2:p + 3], None, OP.subtract))
                assert vcnt[0] == vch_cen(i), (i, vcnt[0])
                v.wait_ge(sch, sch_sq(i))
                vop(v.tensor_scalar(st[:, p + 5:p + 6], st[:, p + 1:p + 2], 1.0 / D, None, OP.mult))
                assert vcnt[0] == vch_var(i), (i, vcnt[0])
                v.wait_ge(sch, sch_sd(i))
                vop(v.reciprocal(st[:, p + 6:p + 7], st[:, p + 6:p + 7]))
                if i >= 2:
                    v.wait_ge(sTW0 if i % 2 == 0 else sTW1, 16 * (i // 2))
                vop(v.tensor_scalar(lnt[:, (i % 2) * D:(i % 2 + 1) * D],
                                    cen[:, (i % 2) * D:(i % 2 + 1) * D],
                                    st[:, p + 6:p + 7], None, OP.mult))
                assert vcnt[0] == vch_norm(i), (i, vcnt[0])

            def evac(i):
                grp = i // SLOTS
                if grp >= 1:
                    v.wait_ge(sRGg, 16 * gcum[grp])
                    v.wait_ge(sRGs, 16 * scum[grp])
                slot = i % SLOTS
                v.wait_ge(tch, tch_A(i))
                for k, go in enumerate(goffA(i)):
                    vop(v.tensor_copy(ring[:, slot * G3 + go:slot * G3 + go + 512],
                                      psA[:, k * 512:(k + 1) * 512]))
                v.wait_ge(tch, tch_B(i))
                for k, go in enumerate(goffB(i)):
                    vop(v.tensor_copy(ring[:, slot * G3 + go:slot * G3 + go + 512],
                                      psB[:, k * 512:(k + 1) * 512]))
                assert vcnt[0] == vch_evac(i), (i, vcnt[0])

            for i in range(NTC):
                ln_v(i)
            for k in range(NTC):
                ln_v(NTC + k)
                evac(k)
            for k in range(NTC, NT):
                evac(k)

        @block.tensor
        def _(t):
            t.wait_ge(sW, 16 * 8)
            for i in range(NT):
                fmx = fmo if i < NTC else fmf
                if i == 0:
                    t.wait_ge(sFMo, 16 * 8)
                if i == NTC:
                    t.wait_ge(sFMf, 16 * 8)
                tl = i % NTC
                if i >= 1:
                    t.wait_ge(vch, vch_evac(i - 1))
                for ps, goffs in [(psA, goffA(i)), (psB, goffB(i))]:
                    for cc in range(8):
                        statn = fmx[:, cc * T + tl * 128: cc * T + (tl + 1) * 128]
                        for k, go in enumerate(goffs):
                            mm = t.matmul(ps[:, k * 512:(k + 1) * 512], statn,
                                          wq_b[:, cc * G3 + go: cc * G3 + go + 512],
                                          start=(cc == 0), stop=(cc == 7))
                            if cc == 7:
                                mm.then_inc(tch, 1)
    return nc


def _emit_regroup(eng, sh, grp, ring, qwin, kvo, kvf, sem, rowstride=RS):
    own = grp < 2
    g_loc = grp % 2
    src = _ap(ring if isinstance(ring, type(qwin)) else ring[:],
              [[rowstride, 128], [G3, SLOTS], [1, 64]], sh * 64)
    if sh < 16:
        base, sh2 = qwin, sh
    else:
        base, sh2 = (kvo if own else kvf), sh - 16
    dst = _ap(base, [[64, 128], [4 * DF, SLOTS], [1, 64]],
              sh2 * (WQ * DF) + g_loc * 32 * DF)
    eng.dma_start(out=dst, in_=src).then_inc(sem, 16)


def kernel(L0_attn_x, L0_attn_Wqkv, L0_attn_Wproj, L0_ffn_rhs):
    x = np.ascontiguousarray(np.asarray(L0_attn_x, dtype=np.float32))
    wqkvT = np.ascontiguousarray(np.asarray(L0_attn_Wqkv, dtype=np.float32).T)
    wprojT = np.ascontiguousarray(np.asarray(L0_attn_Wproj, dtype=np.float32).T)
    rhsT = np.ascontiguousarray(np.asarray(L0_ffn_rhs, dtype=np.float32).T)

    nc = build_fused()
    in_maps = []
    for c in range(8):
        b, q = c // 4, c % 4
        fq = q ^ 2
        in_maps.append({
            "xq": np.ascontiguousarray(x[b, q * T:(q + 1) * T]),
            "xf": np.ascontiguousarray(x[b, fq * T:(fq + 1) * T]),
            "wqkvT": wqkvT,
            "wprojT": wprojT,
            "rhsT": rhsT,
        })
    res = run_bass_kernel_spmd(nc, in_maps, core_ids=list(range(8)))
    out = np.empty((2, 8192, D), dtype=np.float32)
    for c in range(8):
        b, q = c // 4, c % 4
        out[b, q * T:(q + 1) * T] = np.asarray(res.results[c]["y"])
    return out


def kernel_unfused(L0_attn_x, L0_attn_Wqkv, L0_attn_Wproj, L0_ffn_rhs):
    x = np.ascontiguousarray(np.asarray(L0_attn_x, dtype=np.float32))
    wqkvT = np.ascontiguousarray(np.asarray(L0_attn_Wqkv, dtype=np.float32).T)
    wprojT = np.ascontiguousarray(np.asarray(L0_attn_Wproj, dtype=np.float32).T)
    rhsT = np.ascontiguousarray(np.asarray(L0_ffn_rhs, dtype=np.float32).T)

    nc1 = build_l1()
    in_maps = []
    for c in range(8):
        b, q = c // 4, c % 4
        fq = q ^ 2
        in_maps.append({
            "xq": np.ascontiguousarray(x[b, q * T:(q + 1) * T]),
            "xf": np.ascontiguousarray(x[b, fq * T:(fq + 1) * T]),
            "wqkvT": wqkvT,
        })
    res1 = run_bass_kernel_spmd(nc1, in_maps, core_ids=list(range(8)))

    kvw = []
    for b in range(2):
        r0, r1 = res1.results[4 * b + 0], res1.results[4 * b + 1]
        kvw.append(np.ascontiguousarray(np.concatenate(
            [np.asarray(r0["kvo"]), np.asarray(r1["kvo"]),
             np.asarray(r0["kvf"]), np.asarray(r1["kvf"])], axis=2)))

    nc2 = build_l2()
    in_maps2 = []
    for c in range(8):
        b, q = c // 4, c % 4
        in_maps2.append({
            "kvwin": kvw[b],
            "qwin2": np.asarray(res1.results[c]["qwin"]),
            "xq": np.ascontiguousarray(x[b, q * T:(q + 1) * T]),
            "wprojT": wprojT,
            "rhsT": rhsT,
        })
    res2 = run_bass_kernel_spmd(nc2, in_maps2, core_ids=list(range(8)))
    out = np.empty((2, 8192, D), dtype=np.float32)
    for c in range(8):
        b, q = c // 4, c % 4
        out[b, q * T:(q + 1) * T] = np.asarray(res2.results[c]["y"])
    return out


if __name__ == "__main__":
    pass


def v2_rs(oc):
    return 2 + 8 * oc + 2


def v2_qb(oc):
    return 2 + 8 * (oc + 1)


V2_RSB = 67


def v2_smax(h):
    return 67 + 9 * h + 1


def v2_nb(h):
    return 67 + 9 * h + 2


def v2_recip(h):
    return 67 + 9 * h + 3


def v2_pt(h, k):
    return 67 + 9 * h + 4 + k


def v2_hend(h):
    return 76 + 9 * h


V2_P1END = 211


def v3_nm(tc):
    return 211 + 4 * tc + 1


def v3_norm(tc):
    return 211 + 4 * tc + 3


def v3_end(tc):
    return 211 + 4 * tc + 4


V2_P3END = 275


_V4NM, _V4LN2, _V4CAST = {}, {}, {}
_c = 275 + 1
_V4NM[0] = _c
for _tc in range(NTC):
    _c += 1
    _c += 1
    _V4LN2[_tc] = _c
    if _tc + 1 < NTC:
        _c += 1
        _V4NM[_tc + 1] = _c
    _c += 6
    _V4CAST[_tc] = _c
V2_P4END = _c
assert V2_P4END == 419


def v4_nm(tc):
    return _V4NM[tc]


def v4_ln2(tc):
    return _V4LN2[tc]


def v4_cast(tc):
    return _V4CAST[tc]


def v5_zr(tc):
    return 419 + 9 * tc + 1


def v5_nm(tc):
    return 419 + 9 * tc + 2


def v5_vp(tc):
    return 419 + 9 * tc + 5


def v5_y(tc):
    return 419 + 9 * tc + 9


def s2_sgn(oc):
    return oc + 1


def s2_exp(h):
    return 9 + h


S2_P1END = 24


def s3_sum(tc):
    return 25 + 3 * tc


def s3_sq(tc):
    return 26 + 3 * tc


def s3_sd(tc):
    return 27 + 3 * tc


S2_P3END = 72


def s4_sq(tc):
    return 73 + 5 * tc


def s4_sd(tc):
    return 74 + 5 * tc


def s4_sgn(tc):
    return 75 + 5 * tc


def s4_half(tc):
    return 76 + 5 * tc


def s4_cast(tc):
    return 77 + 5 * tc


S2_P4END = 152


def s5_sq(tc):
    return 153 + 2 * tc


def s5_sd(tc):
    return 154 + 2 * tc


def t2_S(h):
    return 3 + 7 * h


def t2_PT(h, k):
    return 4 + 7 * h + k


def t2_PV(h, fc):
    return 6 + 7 * h + fc


def t2_hend(h):
    return 9 + 7 * h


T2_P1END = 114


def t3(tc):
    return 114 + 2 * (tc + 1)


def t5(tc):
    return 146 + 2 * (tc + 1)


def build_l2(cut=5, n3=NTC, p3mask=15, rep=0):
    nc = bass.Bass("TRN2", target_bir_lowering=False)
    kvwin = nc.dram_tensor("kvwin", [2, H, WIN, DF], BF16, kind="ExternalInput").ap()
    qwin2 = nc.dram_tensor("qwin2", [H, WQ, DF], BF16, kind="ExternalInput").ap()
    xq = nc.dram_tensor("xq", [T, D], F32, kind="ExternalInput").ap()
    wprojT = nc.dram_tensor("wprojT", [D, D], F32, kind="ExternalInput").ap()
    rhsT = nc.dram_tensor("rhsT", [D, D], F32, kind="ExternalInput").ap()
    y = nc.dram_tensor("y", [T, D], F32, kind="ExternalOutput").ap()
    otm = nc.dram_tensor("otm", [T, D], BF16).ap()
    qrt = nc.dram_tensor("qrt", [D, D], BF16).ap()
    qltm = nc.dram_tensor("qltm", [T, D], BF16).ap()
    ident_dram = nc.inline_tensor(np.eye(128, dtype=BFNP), "identc").ap()

    with contextlib.ExitStack() as ES:
        E = ES.enter_context
        wpb = E(nc.sbuf_tensor("wpb", [128, 8 * D], BF16))
        qrs = E(nc.sbuf_tensor("qrs", [128, 8 * D], BF16))
        rsb = E(nc.sbuf_tensor("rsb", [128, D], F32))
        rsT = E(nc.sbuf_tensor("rsT", [128, 8], F32))
        rsrow = E(nc.sbuf_tensor("rsrow", [1, D], F32))
        onesf = E(nc.sbuf_tensor("onesf", [1, 128], F32))
        ident = E(nc.sbuf_tensor("ident", [128, 128], BF16))
        ceps = E(nc.sbuf_tensor("ceps", [128, 1], F32))
        rqt = E(nc.sbuf_tensor("rqt", [128, D], F32))
        sgnb = E(nc.sbuf_tensor("sgnb", [128, D], F32))
        zib = E(nc.sbuf_tensor("zib", [128, D], I32))
        qbb = E(nc.sbuf_tensor("qbb", [128, 2 * D], BF16))
        qts = E(nc.sbuf_tensor("qts", [128, H * 16 * 64], BF16))
        ktb = E(nc.sbuf_tensor("ktb", [128, 2 * 16 * 256], BF16))
        vb = E(nc.sbuf_tensor("vb", [128, 2 * 2 * DF], BF16))
        pb = E(nc.sbuf_tensor("pb", [64, 256], BF16))
        ptb = E(nc.sbuf_tensor("ptb", [128, 128], BF16))
        rsum = E(nc.sbuf_tensor("rsum", [64, 1], F32))
        smax = E(nc.sbuf_tensor("smax", [64, 1], F32))
        nb1 = E(nc.sbuf_tensor("nb1", [64, 1], F32))
        rr = E(nc.sbuf_tensor("rr", [64, 1], F32))
        scr8 = E(nc.sbuf_tensor("scr8", [128, 16 * D], F32))
        ofm = qts
        xt2 = E(nc.sbuf_tensor("xt2", [128, 2 * D], F32))
        st2 = E(nc.sbuf_tensor("st2", [128, 16], F32))
        ln2t = E(nc.sbuf_tensor("ln2t", [128, D], F32))
        lst = E(nc.sbuf_tensor("lst", [128, 16], F32))
        zrt = E(nc.sbuf_tensor("zrt", [128, D], F32))
        nrm = E(nc.sbuf_tensor("nrm", [128, D], F32))
        psS = E(nc.psum_tensor("psS", [64, 256], F32))
        psPT = E(nc.psum_tensor("psPT", [128, 64], BF16))
        psO = E(nc.psum_tensor("psO", [64, DF], F32))
        psP = E(nc.psum_tensor("psP", [128, D], F32))

        ow = scr8[:].bitcast(BF16)
        x1 = scr8

        sRQ0 = E(nc.semaphore("sRQ0"))
        sRQ1 = E(nc.semaphore("sRQ1"))
        sQW0 = E(nc.semaphore("sQW0"))
        sQW1 = E(nc.semaphore("sQW1"))
        sQR = E(nc.semaphore("sQR"))
        sQT = E(nc.semaphore("sQT"))
        sKT0 = E(nc.semaphore("sKT0"))
        sKT1 = E(nc.semaphore("sKT1"))
        sV0 = E(nc.semaphore("sV0"))
        sV1 = E(nc.semaphore("sV1"))
        sW2 = E(nc.semaphore("sW2"))
        sID = E(nc.semaphore("sID"))
        sRR = E(nc.semaphore("sRR"))
        sOG = E(nc.semaphore("sOG"))
        sOF = E(nc.semaphore("sOF"))
        sX20 = E(nc.semaphore("sX20"))
        sX21 = E(nc.semaphore("sX21"))
        sQL0 = E(nc.semaphore("sQL0"))
        sQL1 = E(nc.semaphore("sQL1"))
        sQF = E(nc.semaphore("sQF"))
        sY0 = E(nc.semaphore("sY0"))
        sY1 = E(nc.semaphore("sY1"))
        vch2 = E(nc.semaphore("vch2"))
        sch2 = E(nc.semaphore("sch2"))
        tch2 = E(nc.semaphore("tch2"))
        block = E(nc.Block())

        @block.sync
        def _(sy):
            for oc in range(8):
                if oc >= 1:
                    sy.wait_ge(vch2, 2 + 8 * (oc - 1) + 4)
                    sy.wait_ge(sch2, s2_sgn(oc - 1))
                sy.dma_start(out=rqt[:, :],
                             in_=rhsT[oc * 128:(oc + 1) * 128, :]).then_inc(sRQ0, 16)
                if oc >= 1:
                    j = oc - 1
                    sy.wait_ge(vch2, v2_qb(j))
                    sy.dma_start(out=qrt[j * 128:(j + 1) * 128, :],
                                 in_=qbb[:, (j % 2) * D:(j % 2 + 1) * D]).then_inc(sQW0 if j % 2 == 0 else sQW1, 16)
            sy.wait_ge(vch2, v2_qb(7))
            sy.dma_start(out=qrt[7 * 128:8 * 128, :],
                         in_=qbb[:, D:2 * D]).then_inc(sQW1, 16)
            sy.wait_ge(sQW0, 16 * 4)
            sy.wait_ge(sQW1, 16 * 4)
            for ic in range(8):
                sy.dma_start(out=qrs[:, ic * D:(ic + 1) * D],
                             in_=qrt[:, ic * 128:(ic + 1) * 128], transpose=True).then_inc(sQR, 16)
            if cut < 1:
                sy.wait_ge(sQR, 16 * 8)
                return
            for h in range(H):
                for c in range(16):
                    sy.dma_start(out=qts[:, (h * 16 + c) * 64:(h * 16 + c + 1) * 64],
                                 in_=qwin2[h, :, c * 128:(c + 1) * 128], transpose=True).then_inc(sQT, 16)
            for h in range(H):
                if h >= 2:
                    sy.wait_ge(tch2, t2_S(h - 2))
                par = h % 2
                for c in range(16):
                    sy.dma_start(out=ktb[:, par * 4096 + c * 256:par * 4096 + (c + 1) * 256],
                                 in_=kvwin[0, h, :, c * 128:(c + 1) * 128], transpose=True).then_inc(
                        sKT0 if par == 0 else sKT1, 16)
            if cut < 2:
                return
            sy.wait_ge(sOG, 16 * 16)
            sy.wait_ge(tch2, t2_S(H - 1))
            for cc in range(8):
                sy.dma_start(out=ofm[:, cc * T:(cc + 1) * T],
                             in_=otm[:, cc * 128:(cc + 1) * 128], transpose=True).then_inc(sOF, 16)
            if cut < 3:
                sy.wait_ge(sOF, 16 * 8)
                return
            for tc in (range(n3) if p3mask & 8 else []):
                if tc >= 2:
                    sy.wait_ge(vch2, v3_end(tc - 2))
                sy.dma_start(out=xt2[:, (tc % 2) * D:(tc % 2 + 1) * D],
                             in_=xq[tc * 128:(tc + 1) * 128, :]).then_inc(sX20 if tc % 2 == 0 else sX21, 16)
            if cut < 4:
                sy.wait_ge(sOF, 16 * 8)
                return
            for tc in range(NTC):
                sy.wait_ge(vch2, v4_qb(tc))
                sy.dma_start(out=qltm[tc * 128:(tc + 1) * 128, :],
                             in_=qbb[:, (tc % 2) * D:(tc % 2 + 1) * D]).then_inc(sQL0 if tc % 2 == 0 else sQL1, 16)
            sy.wait_ge(sQL0, 16 * 8)
            sy.wait_ge(sQL1, 16 * 8)
            sy.wait_ge(tch2, t3(NTC - 1))
            for cc in range(8):
                sy.dma_start(out=ofm[:, cc * T:(cc + 1) * T],
                             in_=qltm[:, cc * 128:(cc + 1) * 128], transpose=True).then_inc(sQF, 16)
            if cut < 5:
                sy.wait_ge(sQF, 16 * 8)
                return
            for tc in range(NTC):
                sy.wait_ge(vch2, v5_y(tc))
                sy.dma_start(out=y[tc * 128:(tc + 1) * 128, :],
                             in_=nrm[:, :]).then_inc(sY0, 16)
            sy.wait_ge(sY0, 16 * 16)

        @block.gpsimd
        def _(g):
            for cc in range(8):
                g.dma_start(out=wpb[:, cc * D:(cc + 1) * D],
                            in_=wprojT[cc * 128:(cc + 1) * 128, :]).then_inc(sW2, 16)
            g.dma_start(out=ident[:], in_=ident_dram[:]).then_inc(sID, 16)
            for oc in range(8):
                g.wait_ge(vch2, v2_rs(oc))
                src = _ap(rsT[:], [[8, 128], [1, 1]], oc)
                dst = _ap(rsrow[:], [[1024, 1], [1, 128]], oc * 128)
                g.dma_start(out=dst, in_=src).then_inc(sRR, 16)
            if cut < 1:
                return
            for h in range(H):
                if h >= 2:
                    g.wait_ge(tch2, t2_PV(h - 2, 3))
                par = h % 2
                for wc in range(2):
                    g.dma_start(out=vb[:, par * 2 * DF + wc * DF:par * 2 * DF + (wc + 1) * DF],
                                in_=kvwin[1, h, wc * 128:(wc + 1) * 128, :]).then_inc(
                        sV0 if par == 0 else sV1, 16)
            if cut < 2:
                return
            OWF = H * DF
            for h in range(H):
                g.wait_ge(vch2, v2_hend(h))
                src = _ap(ow[:], [[OWF, 64], [64, 32], [1, 64]], h * DF)
                dst = _ap(otm[:], [[32 * D, 64], [D, 32], [1, 64]], h * 64)
                g.dma_start(out=dst, in_=src).then_inc(sOG, 16)

        @block.scalar
        def _(s):
            scnt = [0]

            def sop(inst):
                inst.then_inc(sch2, 1)
                scnt[0] += 1
                s.wait_ge(sch2, scnt[0])

            for oc in range(8):
                s.wait_ge(sRQ0, 16 * (oc + 1))
                if oc >= 1:
                    s.wait_ge(vch2, 2 + 8 * (oc - 1) + 5)
                sop(s.activation(sgnb[:, :], rqt[:, :], AF.Sign))
                assert scnt[0] == s2_sgn(oc)
            if cut < 1:
                return
            for h in range(H):
                s.wait_ge(tch2, t2_S(h))
                s.wait_ge(vch2, v2_nb(h))
                if h >= 1:
                    s.wait_ge(tch2, t2_PT(h - 1, 1))
                    s.wait_ge(vch2, v2_recip(h - 1))
                sop(s.activation(pb[:, :], psS[:, :], AF.Exp, scale=0.125, bias=nb1[:, :],
                                 accum_out=rsum[:, :]))
                assert scnt[0] == s2_exp(h)
            if cut < 3:
                return
            for tc in (range(n3) if p3mask & 2 else []):
                s.wait_ge(vch2, v3_cen(tc))
                sop(s.activation(ln2t[:, :], zrt[:, :], AF.Square, accum_out=st2[:, 1:2]))
                s.wait_ge(vch2, v3_var(tc))
                sop(s.activation(st2[:, 6:7], st2[:, 5:6], AF.Sqrt, bias=ceps[:, 0:1]))
                assert scnt[0] == s3_sd(tc)
            if cut < 4:
                return
            for tc in range(NTC):
                s.wait_ge(vch2, v4_cen(tc))
                sop(s.activation(zrt[:, :], ln2t[:, :], AF.Square,
                                 accum_out=st2[:, 9:10]))
                s.wait_ge(vch2, v4_var(tc))
                sop(s.activation(st2[:, 14:15], st2[:, 13:14], AF.Sqrt, bias=ceps[:, 0:1]))
                s.wait_ge(vch2, v4_ln2(tc))
                if tc >= 1:
                    s.wait_ge(vch2, 323 + 15 * (tc - 1) + 10)
                else:
                    s.wait_ge(vch2, 2 + 8 * 7 + 5)
                sop(s.activation(sgnb[:, :], ln2t[:, :], AF.Sign))
                assert scnt[0] == s4_sgn(tc)
            if cut < 5:
                return
            s.wait_ge(sQL0, 16 * 8)
            s.wait_ge(sQL1, 16 * 8)
            for tc in range(NTC):
                s.wait_ge(vch2, v5_cen(tc))
                sop(s.activation(qbb[:, 0:D], ln2t[:, :], AF.Square, accum_out=st2[:, 9:10]))
                s.wait_ge(vch2, v5_vp(tc))
                sop(s.activation(st2[:, 14:15], st2[:, 13:14], AF.Sqrt, bias=ceps[:, 0:1]))
                assert scnt[0] == s5_sd(tc)

        @block.tensor
        def _(t):
            tcnt = [0]

            def top(inst):
                inst.then_inc(tch2, 1)
                tcnt[0] += 1

            t.wait_ge(vch2, 2)
            t.wait_ge(sRR, 16 * 8)
            top(t.matmul(psP[:, 0:512], onesf[:, :], rsrow[:, 0:512], start=True, stop=True))
            top(t.matmul(psP[:, 512:1024], onesf[:, :], rsrow[:, 512:1024], start=True, stop=True))
            if cut < 1:
                return
            for h in range(H):
                par = h % 2
                if h == 0:
                    t.wait_ge(sQT, 16 * 256)
                t.wait_ge(sKT0 if par == 0 else sKT1, 16 * 16 * (h // 2 + 1))
                if h >= 1:
                    t.wait_ge(sch2, s2_exp(h - 1))
                for c in range(16):
                    mm = t.matmul(psS[:, :], qts[:, (h * 16 + c) * 64:(h * 16 + c + 1) * 64],
                                  ktb[:, par * 4096 + c * 256:par * 4096 + (c + 1) * 256],
                                  start=(c == 0), stop=(c == 15))
                    if c == 15:
                        top(mm)
                assert tcnt[0] == t2_S(h)
                t.wait_ge(sch2, s2_exp(h))
                t.wait_ge(sID, 16)
                if h >= 1:
                    t.wait_ge(vch2, v2_pt(h - 1, 1))
                top(t.transpose(psPT[:, :], pb[:, 0:128], ident[0:64, 0:64]))
                t.wait_ge(vch2, v2_pt(h, 0))
                top(t.transpose(psPT[:, :], pb[:, 128:256], ident[0:64, 0:64]))
                t.wait_ge(sV0 if par == 0 else sV1, 16 * 2 * (h // 2 + 1))
                t.wait_ge(vch2, v2_pt(h, 1))
                if h >= 1:
                    t.wait_ge(vch2, v2_hend(h - 1))
                for fc in range(4):
                    for wc in range(2):
                        mm = t.matmul(psO[:, fc * 512:(fc + 1) * 512],
                                      ptb[:, wc * 64:(wc + 1) * 64],
                                      vb[:, par * 2 * DF + wc * DF + fc * 512:par * 2 * DF + wc * DF + fc * 512 + 512],
                                      start=(wc == 0), stop=(wc == 1))
                        if wc == 1:
                            top(mm)
                assert tcnt[0] == t2_hend(h)
            if cut < 3:
                return
            t.wait_ge(sOF, 16 * 8)
            t.wait_ge(sW2, 16 * 8)
            t.wait_ge(vch2, V2_RSB)
            for tc in (range(n3) if p3mask & 1 else []):
                if tc >= 1:
                    t.wait_ge(vch2, v3_norm(tc - 1))
                    t.wait_ge(sch2, s3_sq(tc - 1))
                for nc2 in range(2):
                    for cc in range(8):
                        mm = t.matmul(psP[:, nc2 * 512:(nc2 + 1) * 512],
                                      ofm[:, cc * T + tc * 128:cc * T + (tc + 1) * 128],
                                      wpb[:, cc * D + nc2 * 512:cc * D + nc2 * 512 + 512],
                                      start=(cc == 0), stop=(cc == 7))
                        if cc == 7:
                            top(mm)
                assert tcnt[0] == t3(tc)
            if cut < 5:
                return
            t.wait_ge(sQF, 16 * 8)
            t.wait_ge(sQR, 16 * 8)
            t.wait_ge(vch2, v3_norm(NTC - 1))
            t.wait_ge(sch2, s3_sq(NTC - 1))
            for tc in range(NTC):
                if tc >= 1:
                    t.wait_ge(vch2, v5_zr(tc - 1))
                    t.wait_ge(sch2, s5_sq(tc - 1))
                for nc2 in range(2):
                    for cc in range(8):
                        mm = t.matmul(psP[:, nc2 * 512:(nc2 + 1) * 512],
                                      ofm[:, cc * T + tc * 128:cc * T + (tc + 1) * 128],
                                      qrs[:, cc * D + nc2 * 512:cc * D + nc2 * 512 + 512],
                                      start=(cc == 0), stop=(cc == 7))
                        if cc == 7:
                            top(mm)
                assert tcnt[0] == t5(tc)

        @block.vector
        def _(v):
            vcnt = [0]

            def vop(inst):
                inst.then_inc(vch2, 1)
                vcnt[0] += 1
                v.wait_ge(vch2, vcnt[0])

            vop(v.memset(onesf[:], 1.0))
            vop(v.memset(ceps[:], EPS))
            for oc in range(8):
                par = oc % 2
                v.wait_ge(sRQ0, 16 * (oc + 1))
                vop(v.reduce_max(st2[:, 0:1], rqt[:, :], axis=AX.X,
                                 apply_absolute_value=True))
                vop(v.tensor_scalar(rsT[:, oc:oc + 1], st2[:, 0:1], 1.0 / 127.0, None, OP.mult))
                assert vcnt[0] == v2_rs(oc)
                vop(v.reciprocal(st2[:, 1:2], rsT[:, oc:oc + 1]))
                vop(v.tensor_scalar(ln2t[:, :], rqt[:, :],
                                    st2[:, 1:2], None, OP.mult))
                v.wait_ge(sch2, s2_sgn(oc))
                vop(v.scalar_tensor_tensor(ln2t[:, :], sgnb[:, :], 0.5, ln2t[:, :], OP.mult, OP.add))
                vop(v.tensor_scalar(ln2t[:, :], ln2t[:, :], 127.49, -127.49, OP.min, OP.max))
                vop(v.tensor_copy(zib[:, :], ln2t[:, :]))
                if oc >= 2:
                    v.wait_ge(sQW0 if par == 0 else sQW1, 16 * (oc // 2))
                vop(v.tensor_copy(qbb[:, par * D:(par + 1) * D], zib[:, :]))
                assert vcnt[0] == v2_qb(oc)
            v.wait_ge(tch2, 2)
            vop(v.tensor_copy(rsb[:, :], psP[:, :]))
            assert vcnt[0] == V2_RSB
            if cut < 1:
                return
            for h in range(H):
                v.wait_ge(tch2, t2_S(h))
                if h >= 1:
                    v.wait_ge(sch2, s2_exp(h - 1))
                vop(v.reduce_max(smax[:, :], psS[:, :], axis=AX.X))
                assert vcnt[0] == v2_smax(h)
                vop(v.tensor_scalar(nb1[:, :], smax[:, :], -0.125, None, OP.mult))
                v.wait_ge(sch2, s2_exp(h))
                vop(v.reciprocal(rr[:, :], rsum[:, :]))
                assert vcnt[0] == v2_recip(h)
                for k in range(2):
                    v.wait_ge(tch2, t2_PT(h, k))
                    vop(v.tensor_copy(ptb[:, k * 64:(k + 1) * 64], psPT[:, :]))
                v.wait_ge(tch2, t2_PV(h, 3))
                for fc in range(4):
                    vop(v.tensor_scalar(ow[0:64, h * DF + fc * 512:h * DF + fc * 512 + 512],
                                        psO[:, fc * 512:(fc + 1) * 512],
                                        rr[:, 0:1], None, OP.mult))
                assert vcnt[0] == v2_hend(h)
            if cut < 3:
                return
            for tc in (range(n3) if p3mask & 4 else []):
                if tc == 0:
                    v.wait_ge(sOG, 16 * 16)
                v.wait_ge(tch2, t3(tc))
                vop(v.reduce_sum(st2[:, 0:1], psP[:, :], axis=AX.X))
                vop(v.tensor_scalar(st2[:, 2:3], st2[:, 0:1], 1.0 / D, None, OP.mult))
                if tc >= 1:
                    v.wait_ge(sch2, s3_sq(tc - 1))
                vop(v.tensor_scalar(zrt[:, :], psP[:, :], st2[:, 2:3], None, OP.subtract))
                assert vcnt[0] == v3_cen(tc)
                v.wait_ge(sch2, s3_sq(tc))
                vop(v.tensor_scalar(st2[:, 5:6], st2[:, 1:2], 1.0 / D, None, OP.mult))
                assert vcnt[0] == v3_var(tc)
                v.wait_ge(sch2, s3_sd(tc))
                vop(v.reciprocal(st2[:, 6:7], st2[:, 6:7]))
                vop(v.tensor_scalar(nrm[:, :], zrt[:, :], st2[:, 6:7], None, OP.mult))
                assert vcnt[0] == v3_norm(tc)
                par = tc % 2
                v.wait_ge(sX20 if par == 0 else sX21, 16 * (tc // 2 + 1))
                vop(v.tensor_tensor(x1[:, tc * D:(tc + 1) * D], nrm[:, :], xt2[:, par * D:(par + 1) * D],
                                    OP.add))
                assert vcnt[0] == v3_end(tc)
            if cut < 4:
                return
            for tc in range(NTC):
                vop(v.reduce_sum(st2[:, 8:9], x1[:, tc * D:(tc + 1) * D], axis=AX.X))
                vop(v.tensor_scalar(st2[:, 10:11], st2[:, 8:9], 1.0 / D, None, OP.mult))
                if tc >= 1:
                    v.wait_ge(sch2, s4_sq(tc - 1))
                vop(v.tensor_scalar(ln2t[:, :], x1[:, tc * D:(tc + 1) * D],
                                    st2[:, 10:11], None, OP.subtract))
                assert vcnt[0] == v4_cen(tc)
                v.wait_ge(sch2, s4_sq(tc))
                vop(v.tensor_scalar(st2[:, 13:14], st2[:, 9:10], 1.0 / D, None, OP.mult))
                assert vcnt[0] == v4_var(tc)
                v.wait_ge(sch2, s4_sd(tc))
                vop(v.reciprocal(st2[:, 14:15], st2[:, 14:15]))
                vop(v.tensor_scalar(ln2t[:, :], ln2t[:, :], st2[:, 14:15], None, OP.mult))
                assert vcnt[0] == v4_ln2(tc)
                vop(v.reduce_max(st2[:, 0:1], ln2t[:, :], axis=AX.X, apply_absolute_value=True))
                vop(v.reciprocal(st2[:, 1:2], st2[:, 0:1]))
                vop(v.tensor_scalar(st2[:, 2:3], st2[:, 1:2], 127.0, None, OP.mult))
                vop(v.tensor_scalar(lst[:, tc:tc + 1], st2[:, 0:1], 1.0 / 127.0, None, OP.mult))
                v.wait_ge(sch2, s4_sgn(tc))
                vop(v.tensor_scalar(ln2t[:, :], ln2t[:, :], st2[:, 2:3], None, OP.mult))
                vop(v.scalar_tensor_tensor(ln2t[:, :], sgnb[:, :], 0.5, ln2t[:, :], OP.mult, OP.add))
                vop(v.tensor_scalar(ln2t[:, :], ln2t[:, :], 127.49, -127.49, OP.min, OP.max))
                vop(v.tensor_copy(zib[:, :], ln2t[:, :]))
                par = tc % 2
                if tc >= 2:
                    v.wait_ge(sQL0 if par == 0 else sQL1, 16 * (tc // 2))
                vop(v.tensor_copy(qbb[:, par * D:(par + 1) * D], zib[:, :]))
                assert vcnt[0] == v4_qb(tc)
            if cut < 5:
                return
            for tc in range(NTC):
                if tc == 0:
                    v.wait_ge(sch2, S2_P4END)
                else:
                    v.wait_ge(sch2, s5_sq(tc - 1))
                v.wait_ge(tch2, t5(tc))
                vop(v.tensor_tensor(zrt[:, :], psP[:, :], rsb[:, :], OP.mult))
                assert vcnt[0] == v5_zr(tc)
                vop(v.reduce_sum(st2[:, 8:9], zrt[:, :], axis=AX.X))
                vop(v.tensor_scalar(st2[:, 10:11], st2[:, 8:9], 1.0 / D, None, OP.mult))
                vop(v.tensor_scalar(ln2t[:, :], zrt[:, :], st2[:, 10:11], None, OP.subtract))
                assert vcnt[0] == v5_cen(tc)
                v.wait_ge(sch2, s5_sq(tc))
                vop(v.tensor_scalar(st2[:, 13:14], st2[:, 9:10], 1.0 / D, None, OP.mult))
                vop(v.tensor_tensor(st2[:, 4:5], lst[:, tc:tc + 1], lst[:, tc:tc + 1], OP.mult))
                vop(v.tensor_tensor(st2[:, 13:14], st2[:, 13:14], st2[:, 4:5], OP.mult))
                assert vcnt[0] == v5_vp(tc)
                v.wait_ge(sch2, s5_sd(tc))
                vop(v.reciprocal(st2[:, 14:15], st2[:, 14:15]))
                vop(v.tensor_tensor(st2[:, 5:6], lst[:, tc:tc + 1], st2[:, 14:15], OP.mult))
                if tc >= 1:
                    v.wait_ge(sY0, 16 * tc)
                vop(v.tensor_scalar(nrm[:, :], ln2t[:, :], st2[:, 5:6], None, OP.mult))
                vop(v.tensor_tensor(nrm[:, :], nrm[:, :],
                                    x1[:, tc * D:(tc + 1) * D], OP.add))
                assert vcnt[0] == v5_y(tc)
    return nc


VT1 = 257
SCH1 = 3 * NT
TCH1 = 160
PRB = 32768
PAW = 24576
RSV = 8 * 1024


def tch_fm(k):
    return 6 * k + 4


def tch_v(k):
    return 6 * k + 6


def tch_ffm(k):
    return 96 + 4 * (k + 1)


def tch_fv(k):
    return 128 + 2 * (k + 1)


def vch_nm(i):
    return 3 * i + 2 if i < NTC else 50 + 9 * (i - NTC)


def vch_norm2(i):
    return 3 * i + 4 if i < NTC else 52 + 9 * (i - NTC)


def vch_fmevac(k):
    return 56 + 9 * k


def vch_evac2(k):
    return 58 + 9 * k


def vch_ffm_e(k):
    return 193 + 4 * (k + 1)


def vch_fv_e(k):
    return 225 + 2 * (k + 1)


def sch_sum(i):
    return 3 * i + 1


def sch_sq2(i):
    return 3 * i + 2


def sch_sd2(i):
    return 3 * i + 3


def build_fused():
    nc = bass.Bass("TRN2", target_bir_lowering=False)
    xq = nc.dram_tensor("xq", [T, D], F32, kind="ExternalInput").ap()
    xf = nc.dram_tensor("xf", [T, D], F32, kind="ExternalInput").ap()
    wqkvT = nc.dram_tensor("wqkvT", [D, G3], F32, kind="ExternalInput").ap()
    wprojT = nc.dram_tensor("wprojT", [D, D], F32, kind="ExternalInput").ap()
    rhsT = nc.dram_tensor("rhsT", [D, D], F32, kind="ExternalInput").ap()
    y = nc.dram_tensor("y", [T, D], F32, kind="ExternalOutput").ap()
    kvo = nc.dram_tensor("kvo", [H, WQ, DF], BF16).ap()
    kvf = nc.dram_tensor("kvf", [H, WQ, DF], BF16).ap()
    kfmP = nc.dram_tensor("kfmP", [2, 8, 128, T], BF16).ap()
    kvsh = nc.dram_tensor("kvsh", [H, WIN, DF], BF16, addr_space="Shared").ap()
    kfmS = nc.dram_tensor("kfmS", [8, 128, 4 * T], BF16, addr_space="Shared").ap()
    ln_tm = nc.dram_tensor("ln_tm", [2, T, D], BF16).ap()
    otm = nc.dram_tensor("otm", [T, D], BF16).ap()
    qrt = nc.dram_tensor("qrt", [D, D], BF16).ap()
    qltm = nc.dram_tensor("qltm", [T, D], BF16).ap()
    cbin = nc.dram_tensor("cbin", [1, 64], BF16).ap()
    cbout = nc.dram_tensor("cbout", [4, 64], BF16).ap()
    ident_dram = nc.inline_tensor(np.eye(128, dtype=BFNP), "identc").ap()

    with contextlib.ExitStack() as ES:
        E = ES.enter_context
        PA = E(nc.sbuf_tensor("PA", [128, PAW], BF16))
        PB = E(nc.sbuf_tensor("PB", [128, 16384], BF16))
        PC = E(nc.sbuf_tensor("PC", [128, 16384], BF16))
        PR = E(nc.sbuf_tensor("PR", [128, 16384], F32))
        PD = E(nc.sbuf_tensor("PD", [128, 7216], F32))
        identT = E(nc.sbuf_tensor("identT", [128, 128], BF16))
        psAT = E(nc.psum_tensor("psAT", [128, 2048], F32))
        psBT = E(nc.psum_tensor("psBT", [128, 2048], F32))

        wq_b = PA[:]
        fmo = PB[:]
        fmf = PB[:]
        PRb = PR[:].bitcast(BF16)
        ringv = PRb
        kfmb = PRb
        qfm = PC[:, 0:16384]
        xt = PD[:, 0:2048]
        cen = PD[:, 2048:4096]
        sqj = PD[:, 4096:5120]
        lnt = PD[:, 5120:6144].bitcast(BF16)
        st = PD[:, 6200:6216]
        ceps = PD[:, 6216:6217]
        psA = psAT[:]
        psB = psBT[:, 0:1024]

        kfr = PA[:, 0:8192]
        vb = PA[:, 8192:16384]
        owb = PA[0:64, 16384:20480]
        wpb = PB[:, 0:8192]
        qrs = PB[:, 8192:16384]
        qbb = PA[:, 20480:22528]
        rsb = PA[:, 22528:24576].bitcast(F32)
        pb = PR[0:64, 0:128].bitcast(BF16)
        ptb = PR[:, 128:192].bitcast(BF16)
        rsrow = PR[0:1, 192:1216]
        onesf_ap = PR[0:1, 1216:1344]
        ofm = PC[:, 0:16384]
        x1 = PR[:]
        rqt = PD[:, 0:1024]
        xsum = PD[:, 0:16]
        sgnb = PD[:, 1024:2048]
        zib = PD[:, 1024:2048].bitcast(I32)
        ln2t = PD[:, 2048:3072]
        zrt = PD[:, 3072:4096]
        nrm = PD[:, 4096:5120]
        smax = PD[0:64, 5120:5121]
        nb1 = PD[0:64, 5121:5122]
        rsum = PD[0:64, 5122:5123]
        rr = PD[0:64, 5123:5124]
        xt2 = PD[:, 5124:7172]
        st2 = PD[:, 7172:7188]
        lst = PD[:, 7188:7204]
        rsT = PD[:, 7204:7212]
        ceps2 = PD[:, 7212:7213]
        ident = identT[:]
        psP = psAT[:, 0:1024]
        psS = psAT[0:64, 1024:1280]
        psPT = psAT[:, 1536:1568].bitcast(BF16)
        psO = psBT[0:64, 0:2048]

        sX0 = E(nc.semaphore("sX0")); sX1 = E(nc.semaphore("sX1"))
        sW = E(nc.semaphore("sW"))
        sTW0 = E(nc.semaphore("sTW0")); sTW1 = E(nc.semaphore("sTW1"))
        sFMo = E(nc.semaphore("sFMo")); sFMf = E(nc.semaphore("sFMf"))
        sRGg = E(nc.semaphore("sRGg")); sRGs = E(nc.semaphore("sRGs"))
        sKW0 = E(nc.semaphore("sKW0")); sKW1 = E(nc.semaphore("sKW1"))
        vch = E(nc.semaphore("vch")); sch = E(nc.semaphore("sch")); tch = E(nc.semaphore("tch"))
        sRL = E(nc.semaphore("sRL")); sBAR = E(nc.semaphore("sBAR")); sCB = E(nc.semaphore("sCB"))
        sRL2 = E(nc.semaphore("sRL2")); sRL3 = E(nc.semaphore("sRL3")); sRL4 = E(nc.semaphore("sRL4"))
        sRG3 = E(nc.semaphore("sRG3")); sKH0 = E(nc.semaphore("sKH0")); sKH1 = E(nc.semaphore("sKH1"))
        gch = E(nc.semaphore("gch"))
        sJ1 = E(nc.semaphore("sJ1")); sJ2 = E(nc.semaphore("sJ2"))
        sRQ0 = E(nc.semaphore("sRQ0"))
        sQW0 = E(nc.semaphore("sQW0")); sQW1 = E(nc.semaphore("sQW1"))
        sQR = E(nc.semaphore("sQR"))
        sKF0 = E(nc.semaphore("sKF0")); sKF1 = E(nc.semaphore("sKF1"))
        sV0 = E(nc.semaphore("sV0")); sV1 = E(nc.semaphore("sV1"))
        sW2 = E(nc.semaphore("sW2")); sID = E(nc.semaphore("sID")); sRR = E(nc.semaphore("sRR"))
        sOG = E(nc.semaphore("sOG")); sOF = E(nc.semaphore("sOF"))
        sX20 = E(nc.semaphore("sX20")); sX21 = E(nc.semaphore("sX21"))
        sQL0 = E(nc.semaphore("sQL0")); sQL1 = E(nc.semaphore("sQL1"))
        sQF = E(nc.semaphore("sQF")); sY0 = E(nc.semaphore("sY0"))
        vch2 = E(nc.semaphore("vch2")); sch2 = E(nc.semaphore("sch2")); tch2 = E(nc.semaphore("tch2"))
        block = E(nc.Block())

        def vregroup(eng, h, grp, sem):
            own = grp < 2
            g_loc = grp % 2
            src = _ap(ringv, [[PRB, 128], [1024, 8], [1, 64]],
                      (0 if own else 12288) + h * 64)
            dst = _ap(kvo if own else kvf, [[64, 128], [4 * DF, 8], [1, 64]],
                      h * (WQ * DF) + g_loc * 32 * DF)
            eng.dma_start(out=dst, in_=src).then_inc(sem, 16)

        def gate_l2(eng):
            eng.wait_ge(vch, VT1); eng.wait_ge(sch, SCH1); eng.wait_ge(tch, TCH1)
            eng.wait_ge(sRGg, 16 * 32); eng.wait_ge(sRGs, 16 * 24); eng.wait_ge(sRG3, 16 * 8)
            eng.wait_ge(sKW0, 16 * 8); eng.wait_ge(sKW1, 16 * 8)
            eng.wait_ge(sTW0, 16 * 16); eng.wait_ge(sTW1, 16 * 16)
            eng.wait_ge(sFMo, 128); eng.wait_ge(sFMf, 128)
            eng.wait_ge(sX0, 256); eng.wait_ge(sX1, 256); eng.wait_ge(sW, 128)

        @block.sync
        def _(sy):
            def load(i):
                src = xq if i < NTC else xf
                tl = i % NTC
                if i >= 2:
                    sy.wait_ge(vch, vch_norm2(i - 2))
                    sy.wait_ge(sch, sch_sq2(i - 2))
                sy.dma_start(out=xt[:, (i % 2) * D:(i % 2 + 1) * D],
                             in_=src[tl * 128:(tl + 1) * 128, :]).then_inc(sX0 if i % 2 == 0 else sX1, 16)

            def write(j):
                sy.wait_ge(vch, vch_norm2(j))
                sy.dma_start(out=ln_tm[j // NTC, (j % NTC) * 128:(j % NTC + 1) * 128, :],
                             in_=lnt[:, (j % 2) * D:(j % 2 + 1) * D]).then_inc(sTW0 if j % 2 == 0 else sTW1, 16)

            for i in range(NTC):
                load(i)
                if i >= 1:
                    write(i - 1)
            write(NTC - 1)
            sy.wait_ge(sTW0, 16 * 8)
            sy.wait_ge(sTW1, 16 * 8)
            for k in range(8):
                sy.dma_start(out=fmo[:, k * T:(k + 1) * T],
                             in_=ln_tm[0, :, k * 128:(k + 1) * 128], transpose=True).then_inc(sFMo, 16)

            def rg_grp(grp, gate):
                sy.wait_ge(vch, gate)
                if grp >= 1:
                    sy.wait_ge(sRGs, 16 * 8 * grp)
                for h in range(1, 16, 2):
                    vregroup(sy, h, grp, sRGs)

            def kfm_write(own_idx, kp, gate):
                sy.wait_ge(vch, gate)
                sy.dma_start(out=kfmP[own_idx, kp],
                             in_=kfmb[:, 8192 + (kp % 2) * 2048:8192 + (kp % 2) * 2048 + 2048]
                             ).then_inc(sKW0 if kp % 2 == 0 else sKW1, 16)

            for i in range(NTC, NT):
                load(i)
                if i >= NTC + 1:
                    write(i - 1)
                if i == NTC + 9:
                    rg_grp(0, vch_evac2(7))
                if i >= NTC + 8:
                    kp = i - NTC - 8
                    kfm_write(0, kp, vch_fmevac(8 + kp))
            write(NT - 1)
            sy.wait_ge(sTW0, 16 * 16)
            sy.wait_ge(sTW1, 16 * 16)
            sy.wait_ge(tch, tch_v(15))
            for k in range(8):
                sy.dma_start(out=fmf[:, k * T:(k + 1) * T],
                             in_=ln_tm[1, :, k * 128:(k + 1) * 128], transpose=True).then_inc(sFMf, 16)
            rg_grp(1, vch_evac2(15))
            for kp in range(8):
                kfm_write(1, kp, vch_ffm_e(kp))
            rg_grp(2, vch_fv_e(7))
            sy.wait_ge(sRGg, 16 * 32)
            sy.wait_ge(sRGs, 16 * 24)
            sy.wait_ge(sRG3, 16 * 8)
            for arm in nc.switch_core_id(engines=[sy], n=128):
                qq = arm.logical % 4
                fqq = qq ^ 2
                sy.dma_start(out=_ap(kvsh, [[WIN * DF, 8], [1, WQ * DF]], fqq * WQ * DF),
                             in_=kvf[0:8]).then_inc(sRL4, 16)
            gate_l2(sy)
            for oc in range(8):
                if oc >= 1:
                    sy.wait_ge(vch2, 2 + 8 * (oc - 1) + 4)
                    sy.wait_ge(sch2, s2_sgn(oc - 1))
                sy.dma_start(out=rqt[:, :],
                             in_=rhsT[oc * 128:(oc + 1) * 128, :]).then_inc(sRQ0, 16)
                if oc >= 1:
                    j = oc - 1
                    sy.wait_ge(vch2, v2_qb(j))
                    sy.dma_start(out=qrt[j * 128:(j + 1) * 128, :],
                                 in_=qbb[:, (j % 2) * D:(j % 2 + 1) * D]).then_inc(sQW0 if j % 2 == 0 else sQW1, 16)
            sy.wait_ge(vch2, v2_qb(7))
            sy.dma_start(out=qrt[7 * 128:8 * 128, :],
                         in_=qbb[:, D:2 * D]).then_inc(sQW1, 16)
            sy.wait_ge(sQW0, 16 * 4)
            sy.wait_ge(sQW1, 16 * 4)
            for ic in range(8):
                sy.dma_start(out=qrs[:, ic * D:(ic + 1) * D],
                             in_=qrt[:, ic * 128:(ic + 1) * 128], transpose=True).then_inc(sQR, 16)
            sy.wait_ge(sBAR, 1)
            for p in range(8):
                if p >= 1:
                    sy.wait_ge(tch2, t2_S(2 * p - 1))
                sy.dma_start(out=kfr[:, 0:4096],
                             in_=kfmS[p][:, 0:4096]).then_inc(sKF0 if p % 2 == 0 else sKF1, 16)
            sy.wait_ge(sOG, 16 * 16)
            sy.wait_ge(tch2, T2_P1END)
            for cc in range(8):
                sy.dma_start(out=ofm[:, cc * T:(cc + 1) * T],
                             in_=otm[:, cc * 128:(cc + 1) * 128], transpose=True).then_inc(sOF, 16)
            for tc in range(NTC):
                if tc >= 2:
                    sy.wait_ge(vch2, v3_end(tc - 2))
                sy.dma_start(out=xt2[:, (tc % 2) * D:(tc % 2 + 1) * D],
                             in_=xq[tc * 128:(tc + 1) * 128, :]).then_inc(sX20 if tc % 2 == 0 else sX21, 16)
            for tc in range(NTC):
                sy.wait_ge(gch, tc + 1)
                sy.dma_start(out=qltm[tc * 128:(tc + 1) * 128, :],
                             in_=qbb[:, (tc % 2) * D:(tc % 2 + 1) * D]).then_inc(sQL0 if tc % 2 == 0 else sQL1, 16)
            sy.wait_ge(sQL0, 16 * 8)
            sy.wait_ge(sQL1, 16 * 8)
            sy.wait_ge(tch2, t3(NTC - 1))
            for cc in range(8):
                sy.dma_start(out=ofm[:, cc * T:(cc + 1) * T],
                             in_=qltm[:, cc * 128:(cc + 1) * 128], transpose=True).then_inc(sQF, 16)
            for tc in range(NTC):
                sy.wait_ge(vch2, v5_y(tc))
                sy.dma_start(out=y[tc * 128:(tc + 1) * 128, :],
                             in_=nrm[:, :]).then_inc(sY0, 16)
            sy.wait_ge(sY0, 16 * 16)

        @block.gpsimd
        def _(g):
            for cc in range(8):
                g.dma_start(out=wq_b[:, cc * G3:(cc + 1) * G3],
                            in_=wqkvT[cc * 128:(cc + 1) * 128, :]).then_inc(sW, 16)
            for grp in range(3):
                gate = vch_evac2(grp * 8 + 7) if grp < 2 else vch_fv_e(grp * 8 - 9)
                g.wait_ge(vch, gate)
                if grp >= 1:
                    g.wait_ge(sRGg, 16 * 8 * grp)
                for h in range(0, 16, 2):
                    vregroup(g, h, grp, sRGg)
            g.wait_ge(sKW0, 16 * 8)
            g.wait_ge(sKW1, 16 * 8)
            for arm in nc.switch_core_id(engines=[g], n=128):
                fqq = (arm.logical % 4) ^ 2
                g.dma_start(out=_ap(kfmS, [[128 * 4 * T, 8], [4 * T, 128], [1, T]], fqq * T),
                            in_=kfmP[1]).then_inc(sRL2, 16)
            g.wait_ge(vch, vch_fv_e(15))
            g.wait_ge(sRGg, 16 * 24)
            for h in range(1, 16, 2):
                vregroup(g, h, 3, sRGg)
            g.wait_ge(sRGg, 16 * 32)
            g.wait_ge(sRGs, 16 * 24)
            g.wait_ge(sRG3, 16 * 8)
            for arm in nc.switch_core_id(engines=[g], n=128):
                fqq = (arm.logical % 4) ^ 2
                g.dma_start(out=_ap(kvsh, [[WIN * DF, 8], [1, WQ * DF]],
                                    8 * WIN * DF + fqq * WQ * DF),
                            in_=kvf[8:16]).then_inc(sRL3, 16)
            gate_l2(g)
            g.wait_ge(sRL, 32)
            g.wait_ge(sRL2, 16)
            g.wait_ge(sRL3, 16)
            g.wait_ge(sRL4, 16)
            g.dma_start(out=cbin[:], in_=ident_dram[0:1, 0:64], allow_slow_non_contiguous=True).then_inc(sCB, 16)
            g.wait_ge(sCB, 16)
            g.collective_compute("AllGather", mybir.AluOpType.bypass,
                                 replica_groups=[[0, 1], [2, 3], [4, 5], [6, 7]],
                                 ins=[cbin[:]], outs=[cbout[0:2]]).then_inc(sBAR, 1)
            for cc in range(8):
                g.dma_start(out=wpb[:, cc * D:(cc + 1) * D],
                            in_=wprojT[cc * 128:(cc + 1) * 128, :]).then_inc(sW2, 16)
            for oc in range(8):
                g.wait_ge(vch2, v2_rs(oc))
                src = _ap(PD[:], [[7216, 128], [1, 1]], 7204 + oc)
                dst = _ap(PR[:], [[16384, 1], [1, 128]], 192 + oc * 128)
                g.dma_start(out=dst, in_=src).then_inc(sRR, 16)
            g.wait_ge(sBAR, 1)

            def vload(h):
                if h >= 2:
                    g.wait_ge(tch2, t2_PV(h - 2, 3))
                par = h % 2
                for wc in range(2):
                    g.dma_start(out=vb[:, par * 2 * DF + wc * DF:par * 2 * DF + (wc + 1) * DF],
                                in_=kvsh[h, wc * 128:(wc + 1) * 128, :]).then_inc(
                        sV0 if par == 0 else sV1, 16)

            vload(0)
            vload(1)
            for h in range(H):
                g.wait_ge(vch2, v2_hend(h))
                if h >= 1:
                    g.wait_ge(sOG, 16 * h)
                src = _ap(PA[:], [[PAW, 64], [64, 32], [1, 64]], 16384 + (h % 2) * DF)
                dst = _ap(otm, [[32 * D, 64], [D, 32], [1, 64]], h * 64)
                g.dma_start(out=dst, in_=src).then_inc(sOG, 16)
                if h + 2 < H:
                    vload(h + 2)
            for tc in range(NTC):
                par = tc % 2
                g.wait_ge(sch2, s4_cast(tc))
                if tc >= 2:
                    g.wait_ge(sQL0 if par == 0 else sQL1, 16 * (tc // 2))
                g.tensor_copy(qbb[:, par * D:(par + 1) * D], zib[:, :]).then_inc(gch, 1)
                g.wait_ge(gch, tc + 1)
            g.wait_ge(sY0, 16 * 16)

        @block.scalar
        def _(s):
            s.dma_start(out=identT[:], in_=ident_dram[:]).then_inc(sID, 16)
            s.wait_ge(vch, 1)
            for i in range(NT):
                p = (i % 2) * 8
                par = (i % 2) * D
                s.wait_ge(sX0 if i % 2 == 0 else sX1, 16 * (i // 2 + 1))
                if i >= 2:
                    s.wait_ge(vch, vch_nm(i - 2))
                s.activation(sqj[:, :], xt[:, par:par + D], AF.Copy,
                             accum_out=st[:, p + 0:p + 1]).then_inc(sch, 1)
                s.wait_ge(vch, vch_nm(i))
                s.activation(sqj[:, :], xt[:, par:par + D], AF.Square,
                             bias=st[:, p + 2:p + 3],
                             accum_out=st[:, p + 1:p + 2]).then_inc(sch, 1)
                s.wait_ge(sch, sch_sq2(i))
                s.activation(st[:, p + 6:p + 7], st[:, p + 1:p + 2], AF.Sqrt,
                             scale=1.0 / D, bias=ceps[:, 0:1]).then_inc(sch, 1)
            s.wait_ge(sRGg, 16 * 16)
            s.wait_ge(sRGs, 16 * 16)
            s.wait_ge(sKW0, 16 * 4)
            s.wait_ge(sKW1, 16 * 4)
            for arm in nc.switch_core_id(engines=[s], n=128):
                qq = arm.logical % 4
                s.dma_start(out=_ap(kvsh, [[WIN * DF, H], [1, WQ * DF]], qq * WQ * DF),
                            in_=kvo[:]).then_inc(sRL, 16)
                s.dma_start(out=_ap(kfmS, [[128 * 4 * T, 8], [4 * T, 128], [1, T]], qq * T),
                            in_=kfmP[0]).then_inc(sRL, 16)
            s.wait_ge(vch, vch_fv_e(15))
            for h in range(0, 16, 2):
                vregroup(s, h, 3, sRG3)
            gate_l2(s)
            scnt = [0]

            def sop(inst):
                inst.then_inc(sch2, 1)
                scnt[0] += 1
                s.wait_ge(sch2, scnt[0])

            for oc in range(8):
                s.wait_ge(sRQ0, 16 * (oc + 1))
                if oc >= 1:
                    s.wait_ge(vch2, v2_qb(oc - 1))
                sop(s.activation(sgnb[:, :], rqt[:, :], AF.Sign))
                assert scnt[0] == s2_sgn(oc)
            s.wait_ge(sBAR, 1)
            s.dma_start(out=kfr[:, 4096:8192],
                        in_=kfmS[0][:, 4096:8192]).then_inc(sKH0, 16)
            for h in range(H):
                s.wait_ge(tch2, t2_S(h))
                s.wait_ge(vch2, v2_nb(h))
                if h >= 1:
                    s.wait_ge(tch2, t2_PT(h - 1, 1))
                    s.wait_ge(vch2, v2_recip(h - 1))
                sop(s.activation(pb[:, :], psS[:, :], AF.Exp, scale=0.125, bias=nb1[:, :],
                                 accum_out=rsum[:, :]))
                assert scnt[0] == s2_exp(h)
                if h % 2 == 1 and h <= 13:
                    p = (h + 1) // 2
                    s.dma_start(out=kfr[:, 4096:8192],
                                in_=kfmS[p][:, 4096:8192]).then_inc(sKH0 if p % 2 == 0 else sKH1, 16)
            for tc in range(NTC):
                s.wait_ge(tch2, t3(tc))
                if tc >= 1:
                    s.wait_ge(vch2, v3_norm(tc - 1))
                sop(s.activation(zrt[:, :], psP[:, :], AF.Copy, accum_out=st2[:, 0:1]))
                s.wait_ge(vch2, v3_nm(tc))
                sop(s.activation(ln2t[:, :], zrt[:, :], AF.Square, bias=st2[:, 2:3],
                                 accum_out=st2[:, 1:2]))
                s.wait_ge(sch2, s3_sq(tc))
                sop(s.activation(st2[:, 6:7], st2[:, 1:2], AF.Sqrt, scale=1.0 / D,
                                 bias=ceps2[:, 0:1]))
                assert scnt[0] == s3_sd(tc)
            for tc in range(NTC):
                s.wait_ge(vch2, v4_nm(tc))
                pns = 10 + (tc % 2)
                sop(s.activation(zrt[:, :], x1[:, tc * D:(tc + 1) * D], AF.Square,
                                 bias=st2[:, pns:pns + 1], accum_out=st2[:, 9:10]))
                s.wait_ge(sch2, s4_sq(tc))
                sop(s.activation(st2[:, 14:15], st2[:, 9:10], AF.Sqrt, scale=1.0 / D,
                                 bias=ceps2[:, 0:1]))
                s.wait_ge(vch2, v4_ln2(tc))
                if tc >= 1:
                    s.wait_ge(gch, tc)
                else:
                    s.wait_ge(vch2, v2_qb(7))
                sop(s.activation(sgnb[:, :], ln2t[:, :], AF.Sign))
                assert scnt[0] == s4_sgn(tc)
                sop(s.mul(sgnb[:, :], sgnb[:, :], 0.5))
                assert scnt[0] == s4_half(tc)
                s.wait_ge(vch2, v4_cast(tc))
                if tc >= 1:
                    s.wait_ge(gch, tc)
                sop(s.activation(zib[:, :], ln2t[:, :], AF.Copy))
                assert scnt[0] == s4_cast(tc)
            s.wait_ge(sQL0, 16 * 8)
            s.wait_ge(sQL1, 16 * 8)
            for tc in range(NTC):
                s.wait_ge(vch2, v5_nm(tc))
                sop(s.activation(qbb[:, 0:D], zrt[:, :], AF.Square, bias=st2[:, 10:11],
                                 accum_out=st2[:, 9:10]))
                s.wait_ge(vch2, v5_vp(tc))
                sop(s.activation(st2[:, 14:15], st2[:, 13:14], AF.Sqrt, bias=ceps2[:, 0:1]))
                assert scnt[0] == s5_sd(tc)
            s.wait_ge(sY0, 16 * 16)

        @block.tensor
        def _(t):
            t.wait_ge(sID, 16)
            for _j in range(250):
                mm = t.matmul(psBT[:, 0:128], ident[0:1, 0:128], ident[0:1, 0:128],
                              start=True, stop=True)
            mm.then_inc(sJ1, 1)
            t.wait_ge(sW, 16 * 8)
            tcnt1 = [0]

            def top1(inst):
                inst.then_inc(tch, 1)
                tcnt1[0] += 1

            t.wait_ge(sFMo, 16 * 8)
            for k in range(NTC):
                if k >= 1:
                    t.wait_ge(vch, vch_fmevac(k - 1))
                for cc in range(8):
                    statn = wq_b[:, cc * G3 + k * 128: cc * G3 + (k + 1) * 128]
                    for ts in range(4):
                        mm = t.matmul(psA[:, ts * 512:(ts + 1) * 512], statn,
                                      fmo[:, cc * T + ts * 512: cc * T + (ts + 1) * 512],
                                      start=(cc == 0), stop=(cc == 7))
                        if cc == 7:
                            top1(mm)
                assert tcnt1[0] == tch_fm(k)
                if k >= 1:
                    t.wait_ge(vch, vch_evac2(k - 1))
                else:
                    t.wait_ge(sJ1, 1)
                for cc in range(8):
                    statn = fmo[:, cc * T + k * 128: cc * T + (k + 1) * 128]
                    for vc in range(2):
                        mm = t.matmul(psB[:, vc * 512:(vc + 1) * 512], statn,
                                      wq_b[:, cc * G3 + 2048 + vc * 512: cc * G3 + 2048 + (vc + 1) * 512],
                                      start=(cc == 0), stop=(cc == 7))
                        if cc == 7:
                            top1(mm)
                assert tcnt1[0] == tch_v(k)
            t.wait_ge(sFMf, 16 * 8)
            for k in range(8):
                t.wait_ge(vch, vch_fmevac(15) if k == 0 else vch_ffm_e(k - 1))
                for cc in range(8):
                    statn = wq_b[:, cc * G3 + 1024 + k * 128: cc * G3 + 1024 + (k + 1) * 128]
                    for ts in range(4):
                        mm = t.matmul(psA[:, ts * 512:(ts + 1) * 512], statn,
                                      fmf[:, cc * T + ts * 512: cc * T + (ts + 1) * 512],
                                      start=(cc == 0), stop=(cc == 7))
                        if cc == 7:
                            top1(mm)
                assert tcnt1[0] == tch_ffm(k)
            for k in range(NTC):
                t.wait_ge(vch, vch_evac2(15) if k == 0 else vch_fv_e(k - 1))
                for cc in range(8):
                    statn = fmf[:, cc * T + k * 128: cc * T + (k + 1) * 128]
                    for vc in range(2):
                        mm = t.matmul(psB[:, vc * 512:(vc + 1) * 512], statn,
                                      wq_b[:, cc * G3 + 2048 + vc * 512: cc * G3 + 2048 + (vc + 1) * 512],
                                      start=(cc == 0), stop=(cc == 7))
                        if cc == 7:
                            top1(mm)
                assert tcnt1[0] == tch_fv(k)
            t.wait_ge(vch, 257)
            for _j in range(400):
                mm = t.matmul(psBT[:, 0:128], ident[0:1, 0:128], ident[0:1, 0:128],
                              start=True, stop=True)
            mm.then_inc(sJ2, 1)
            gate_l2(t)
            tcnt = [0]

            def top(inst):
                inst.then_inc(tch2, 1)
                tcnt[0] += 1

            t.wait_ge(vch2, 2)
            t.wait_ge(sRR, 16 * 8)
            top(t.matmul(psP[:, 0:512], onesf_ap, rsrow[:, 0:512], start=True, stop=True))
            top(t.matmul(psP[:, 512:1024], onesf_ap, rsrow[:, 512:1024], start=True, stop=True))
            for h in range(H):
                p = h // 2
                rB = (h % 2) * 64
                t.wait_ge(sKF0 if p % 2 == 0 else sKF1, 16 * (p // 2 + 1))
                t.wait_ge(sKH0 if p % 2 == 0 else sKH1, 16 * (p // 2 + 1))
                if h >= 1:
                    t.wait_ge(sch2, s2_exp(h - 1))
                for j in range(32):
                    qsl = _ap(qfm, [[16384, 64], [32, 64]], rB * 16384 + p * 2048 + j)
                    ksl = _ap(kfr, [[PAW, 64], [32, 256]], rB * PAW + j)
                    mm = t.matmul(psS[:, :], qsl, ksl, start=(j == 0), stop=(j == 31))
                    if j == 31:
                        top(mm)
                assert tcnt[0] == t2_S(h)
                t.wait_ge(sch2, s2_exp(h))
                t.wait_ge(sID, 16)
                if h >= 1:
                    t.wait_ge(vch2, v2_pt(h - 1, 1))
                top(t.transpose(psPT[:, :], pb[:, 0:128], ident[0:64, 0:64]))
                t.wait_ge(vch2, v2_pt(h, 0))
                top(t.transpose(psPT[:, :], pb[:, 128:256], ident[0:64, 0:64]))
                par = h % 2
                if h == 0:
                    t.wait_ge(sJ2, 1)
                t.wait_ge(sV0 if par == 0 else sV1, 16 * 2 * (h // 2 + 1))
                t.wait_ge(vch2, v2_pt(h, 1))
                if h >= 1:
                    t.wait_ge(vch2, v2_hend(h - 1))
                for fc in range(4):
                    for wc in range(2):
                        mm = t.matmul(psO[:, fc * 512:(fc + 1) * 512],
                                      ptb[:, wc * 64:(wc + 1) * 64],
                                      vb[:, par * 2 * DF + wc * DF + fc * 512:par * 2 * DF + wc * DF + fc * 512 + 512],
                                      start=(wc == 0), stop=(wc == 1))
                        if wc == 1:
                            top(mm)
                assert tcnt[0] == t2_hend(h)
            t.wait_ge(sOF, 16 * 8)
            t.wait_ge(sW2, 16 * 8)
            t.wait_ge(vch2, V2_RSB)
            for tc in range(NTC):
                if tc >= 1:
                    t.wait_ge(sch2, s3_sum(tc - 1))
                for nc2 in range(2):
                    for cc in range(8):
                        mm = t.matmul(psP[:, nc2 * 512:(nc2 + 1) * 512],
                                      ofm[:, cc * T + tc * 128:cc * T + (tc + 1) * 128],
                                      wpb[:, cc * D + nc2 * 512:cc * D + nc2 * 512 + 512],
                                      start=(cc == 0), stop=(cc == 7))
                        if cc == 7:
                            top(mm)
                assert tcnt[0] == t3(tc)
            t.wait_ge(sQF, 16 * 8)
            t.wait_ge(sQR, 16 * 8)
            t.wait_ge(sch2, s3_sum(NTC - 1))
            for tc in range(NTC):
                if tc >= 1:
                    t.wait_ge(vch2, v5_zr(tc - 1))
                for nc2 in range(2):
                    for cc in range(8):
                        mm = t.matmul(psP[:, nc2 * 512:(nc2 + 1) * 512],
                                      ofm[:, cc * T + tc * 128:cc * T + (tc + 1) * 128],
                                      qrs[:, cc * D + nc2 * 512:cc * D + nc2 * 512 + 512],
                                      start=(cc == 0), stop=(cc == 7))
                        if cc == 7:
                            top(mm)
                assert tcnt[0] == t5(tc)

        @block.vector
        def _(v):
            vcnt = [0]

            def vop(inst):
                inst.then_inc(vch, 1)
                vcnt[0] += 1
                v.wait_ge(vch, vcnt[0])

            vop(v.memset(ceps[:], EPS))

            def ln_v(i):
                p = (i % 2) * 8
                v.wait_ge(sch, sch_sum(i))
                vop(v.tensor_scalar(st[:, p + 2:p + 3], st[:, p + 0:p + 1], -1.0 / D, None, OP.mult))
                assert vcnt[0] == vch_nm(i), (i, vcnt[0])
                v.wait_ge(sch, sch_sd2(i))
                vop(v.reciprocal(st[:, p + 6:p + 7], st[:, p + 6:p + 7]))
                if i >= 2:
                    v.wait_ge(sTW0 if i % 2 == 0 else sTW1, 16 * (i // 2))
                vop(v.tensor_scalar(lnt[:, (i % 2) * D:(i % 2 + 1) * D],
                                    xt[:, (i % 2) * D:(i % 2 + 1) * D],
                                    st[:, p + 2:p + 3], st[:, p + 6:p + 7], OP.add, OP.mult))
                assert vcnt[0] == vch_norm2(i), (i, vcnt[0])

            def fm_evac(k, own):
                v.wait_ge(tch, tch_fm(k) if own else tch_ffm(k))
                if own and k < 8:
                    dst = lambda ts: qfm[:, k * 2048 + ts * 512: k * 2048 + (ts + 1) * 512]
                else:
                    kp = (k - 8) if own else k
                    nprev = (kp // 2) if own else (4 + kp // 2)
                    if nprev > 0:
                        v.wait_ge(sKW0 if kp % 2 == 0 else sKW1, 16 * nprev)
                    dst = lambda ts: kfmb[:, 8192 + (kp % 2) * 2048 + ts * 512: 8192 + (kp % 2) * 2048 + (ts + 1) * 512]
                for ts in range(4):
                    vop(v.tensor_copy(dst(ts), psA[:, ts * 512:(ts + 1) * 512]))

            def v_evac(k, own):
                if own and k >= 8:
                    v.wait_ge(sRGg, 16 * 8)
                    v.wait_ge(sRGs, 16 * 8)
                if (not own) and k >= 8:
                    v.wait_ge(sRGg, 16 * 24)
                    v.wait_ge(sRGs, 16 * 24)
                v.wait_ge(tch, tch_v(k) if own else tch_fv(k))
                slot = k % 8
                base = 0 if own else 12288
                for vc in range(2):
                    vop(v.tensor_copy(ringv[:, base + slot * 1024 + vc * 512: base + slot * 1024 + (vc + 1) * 512],
                                      psB[:, vc * 512:(vc + 1) * 512]))

            for i in range(NTC):
                ln_v(i)
            for k in range(NTC):
                ln_v(NTC + k)
                fm_evac(k, True)
                v_evac(k, True)
                assert vcnt[0] == vch_evac2(k), (k, vcnt[0])
            for k in range(8):
                fm_evac(k, False)
                assert vcnt[0] == vch_ffm_e(k), (k, vcnt[0])
            for k in range(NTC):
                v_evac(k, False)
                assert vcnt[0] == vch_fv_e(k), (k, vcnt[0])
            gate_l2(v)
            vcnt2 = [0]

            def vop2(inst):
                inst.then_inc(vch2, 1)
                vcnt2[0] += 1
                v.wait_ge(vch2, vcnt2[0])

            vop2(v.memset(onesf_ap, 1.0))
            vop2(v.memset(ceps2, EPS))
            for oc in range(8):
                v.wait_ge(sRQ0, 16 * (oc + 1))
                vop2(v.reduce_max(st2[:, 0:1], rqt[:, :], axis=AX.X,
                                  apply_absolute_value=True))
                vop2(v.tensor_scalar(rsT[:, oc:oc + 1], st2[:, 0:1], 1.0 / 127.0, None, OP.mult))
                assert vcnt2[0] == v2_rs(oc)
                vop2(v.reciprocal(st2[:, 1:2], rsT[:, oc:oc + 1]))
                vop2(v.tensor_scalar(ln2t[:, :], rqt[:, :],
                                     st2[:, 1:2], None, OP.mult))
                v.wait_ge(sch2, s2_sgn(oc))
                vop2(v.scalar_tensor_tensor(ln2t[:, :], sgnb[:, :], 0.5, ln2t[:, :], OP.mult, OP.add))
                vop2(v.tensor_scalar(ln2t[:, :], ln2t[:, :], 127.49, -127.49, OP.min, OP.max))
                vop2(v.tensor_copy(zib[:, :], ln2t[:, :]))
                if oc >= 2:
                    v.wait_ge(sQW0 if oc % 2 == 0 else sQW1, 16 * (oc // 2))
                vop2(v.tensor_copy(qbb[:, (oc % 2) * D:(oc % 2 + 1) * D], zib[:, :]))
                assert vcnt2[0] == v2_qb(oc)
            v.wait_ge(tch2, 2)
            vop2(v.tensor_copy(rsb[:, :], psP[:, :]))
            assert vcnt2[0] == V2_RSB
            for h in range(H):
                v.wait_ge(tch2, t2_S(h))
                if h >= 1:
                    v.wait_ge(sch2, s2_exp(h - 1))
                vop2(v.reduce_max(smax[:, :], psS[:, :], axis=AX.X))
                assert vcnt2[0] == v2_smax(h)
                vop2(v.tensor_scalar(nb1[:, :], smax[:, :], -0.125, None, OP.mult))
                v.wait_ge(sch2, s2_exp(h))
                vop2(v.reciprocal(rr[:, :], rsum[:, :]))
                assert vcnt2[0] == v2_recip(h)
                for k in range(2):
                    v.wait_ge(tch2, t2_PT(h, k))
                    vop2(v.tensor_copy(ptb[:, k * 64:(k + 1) * 64], psPT[:, :]))
                v.wait_ge(tch2, t2_PV(h, 3))
                if h >= 2:
                    v.wait_ge(sOG, 16 * (h - 1))
                for fc in range(4):
                    vop2(v.tensor_scalar(owb[:, (h % 2) * DF + fc * 512:(h % 2) * DF + fc * 512 + 512],
                                         psO[:, fc * 512:(fc + 1) * 512],
                                         rr[:, 0:1], None, OP.mult))
                assert vcnt2[0] == v2_hend(h)
            for tc in range(NTC):
                if tc == 0:
                    v.wait_ge(sOG, 16 * 16)
                v.wait_ge(sch2, s3_sum(tc))
                vop2(v.tensor_scalar(st2[:, 2:3], st2[:, 0:1], -1.0 / D, None, OP.mult))
                assert vcnt2[0] == v3_nm(tc)
                v.wait_ge(sch2, s3_sd(tc))
                vop2(v.reciprocal(st2[:, 6:7], st2[:, 6:7]))
                vop2(v.tensor_scalar(nrm[:, :], zrt[:, :], st2[:, 2:3], st2[:, 6:7],
                                     OP.add, OP.mult))
                assert vcnt2[0] == v3_norm(tc)
                par = tc % 2
                v.wait_ge(sX20 if par == 0 else sX21, 16 * (tc // 2 + 1))
                vop2(v.scalar_tensor_tensor(x1[:, tc * D:(tc + 1) * D], nrm[:, :], 0.0,
                                            xt2[:, par * D:(par + 1) * D], OP.add, OP.add,
                                            accum_out=xsum[:, tc:tc + 1]))
                assert vcnt2[0] == v3_end(tc)
            vop2(v.tensor_scalar(st2[:, 10:11], xsum[:, 0:1], -1.0 / D, None, OP.mult))
            assert vcnt2[0] == v4_nm(0)
            for tc in range(NTC):
                pn = 10 + (tc % 2)
                v.wait_ge(sch2, s4_sd(tc))
                vop2(v.reciprocal(st2[:, 14:15], st2[:, 14:15]))
                vop2(v.tensor_scalar(ln2t[:, :], x1[:, tc * D:(tc + 1) * D],
                                     st2[:, pn:pn + 1], st2[:, 14:15], OP.add, OP.mult))
                assert vcnt2[0] == v4_ln2(tc)
                if tc + 1 < NTC:
                    pn1 = 10 + ((tc + 1) % 2)
                    if tc >= 1:
                        v.wait_ge(sch2, s4_sq(tc - 1))
                    vop2(v.tensor_scalar(st2[:, pn1:pn1 + 1], xsum[:, tc + 1:tc + 2],
                                         -1.0 / D, None, OP.mult))
                    assert vcnt2[0] == v4_nm(tc + 1)
                vop2(v.reduce_max(st2[:, 0:1], ln2t[:, :], axis=AX.X, apply_absolute_value=True))
                vop2(v.reciprocal(st2[:, 1:2], st2[:, 0:1]))
                vop2(v.tensor_scalar(st2[:, 2:3], st2[:, 1:2], 127.0, None, OP.mult))
                vop2(v.tensor_scalar(lst[:, tc:tc + 1], st2[:, 0:1], 1.0 / 127.0, None, OP.mult))
                v.wait_ge(sch2, s4_half(tc))
                vop2(v.scalar_tensor_tensor(ln2t[:, :], ln2t[:, :], st2[:, 2:3], sgnb[:, :],
                                            OP.mult, OP.add))
                vop2(v.tensor_scalar(ln2t[:, :], ln2t[:, :], 127.49, -127.49, OP.min, OP.max))
                assert vcnt2[0] == v4_cast(tc)
            for tc in range(NTC):
                if tc == 0:
                    v.wait_ge(sch2, S2_P4END)
                else:
                    v.wait_ge(sch2, s5_sq(tc - 1))
                v.wait_ge(tch2, t5(tc))
                vop2(v.scalar_tensor_tensor(zrt[:, :], psP[:, :], 0.0, rsb[:, :],
                                            OP.add, OP.mult, accum_out=st2[:, 8:9]))
                assert vcnt2[0] == v5_zr(tc)
                vop2(v.tensor_scalar(st2[:, 10:11], st2[:, 8:9], -1.0 / D, None, OP.mult))
                assert vcnt2[0] == v5_nm(tc)
                vop2(v.tensor_tensor(st2[:, 4:5], lst[:, tc:tc + 1], lst[:, tc:tc + 1], OP.mult))
                v.wait_ge(sch2, s5_sq(tc))
                vop2(v.tensor_scalar(st2[:, 13:14], st2[:, 9:10], 1.0 / D, None, OP.mult))
                vop2(v.tensor_tensor(st2[:, 13:14], st2[:, 13:14], st2[:, 4:5], OP.mult))
                assert vcnt2[0] == v5_vp(tc)
                v.wait_ge(sch2, s5_sd(tc))
                vop2(v.reciprocal(st2[:, 14:15], st2[:, 14:15]))
                vop2(v.tensor_tensor(st2[:, 5:6], lst[:, tc:tc + 1], st2[:, 14:15], OP.mult))
                if tc >= 1:
                    v.wait_ge(sY0, 16 * tc)
                vop2(v.tensor_scalar(nrm[:, :], zrt[:, :], st2[:, 10:11], st2[:, 5:6],
                                     OP.add, OP.mult))
                vop2(v.tensor_tensor(nrm[:, :], nrm[:, :],
                                     x1[:, tc * D:(tc + 1) * D], OP.add))
                assert vcnt2[0] == v5_y(tc)
    return nc


def kernel(L0_attn_x, L0_attn_Wqkv, L0_attn_Wproj, L0_ffn_rhs):
    x = np.ascontiguousarray(np.asarray(L0_attn_x, dtype=np.float32))
    wqkvT = np.ascontiguousarray(np.asarray(L0_attn_Wqkv, dtype=np.float32).T)
    wprojT = np.ascontiguousarray(np.asarray(L0_attn_Wproj, dtype=np.float32).T)
    rhsT = np.ascontiguousarray(np.asarray(L0_ffn_rhs, dtype=np.float32).T)

    nc = build_fused()
    in_maps = []
    for c in range(8):
        b, q = c // 4, c % 4
        fq = q ^ 2
        in_maps.append({
            "xq": np.ascontiguousarray(x[b, q * T:(q + 1) * T]),
            "xf": np.ascontiguousarray(x[b, fq * T:(fq + 1) * T]),
            "wqkvT": wqkvT,
            "wprojT": wprojT,
            "rhsT": rhsT,
        })
    res = run_bass_kernel_spmd(nc, in_maps, core_ids=list(range(8)))
    out = np.empty((2, 8192, D), dtype=np.float32)
    for c in range(8):
        b, q = c // 4, c % 4
        out[b, q * T:(q + 1) * T] = np.asarray(res.results[c]["y"])
    return out
